# version 11
# baseline (speedup 1.0000x reference)
import numpy as np
import ml_dtypes
from contextlib import ExitStack

import concourse.bass as bass
import concourse.mybir as mybir
from concourse.bass_utils import run_bass_kernel_spmd

F32 = mybir.dt.float32
BF16 = mybir.dt.bfloat16
I8 = mybir.dt.int8
AF = mybir.ActivationFunctionType
ALU = mybir.AluOpType
AX = mybir.AxisListType

D = 4096
DC = 32
TKV = 4096
TQ = 2048
PART = 2048
NCT = 16
NI = 11008
NIT = 86
EPS = 1e-5
VW = 272
BIGC = 20000.0

B_MODE = "full"
PROJ_LIMIT = 10 ** 9


def _bs_tiles(bs):
    if bs < 2:
        return ([("K", ct, 16 + ct) for ct in range(NCT)]
                + [("V", ct, 32 + ct) for ct in range(NCT)])
    return [("Q", ct, ct) for ct in range(NCT)]


def _sched():
    tab = []
    for sl in range(32):
        hh, qs = sl // 4, sl % 4
        nks = min(qs + 5, 8)
        nkc = min(4 * qs + 20, 32)
        band0 = 4 * qs
        nav = [min(4 * qs + qt + 17, 32) for qt in range(4)]
        tab.append(dict(hh=hh, qs=qs, nks=nks, nkc=nkc, band0=band0,
                        nband=nkc - band0, nav=nav))
    return tab


SCHED = _sched()
EX_PER = [s["nkc"] for s in SCHED]
AV_PER = [4] * 32
OT_PER = [8] * 32


def _cum(per):
    c = [0]
    for v in per:
        c.append(c[-1] + v)
    return c


EX_CUM = _cum(EX_PER)
AV_CUM = _cum(AV_PER)
OT_CUM = _cum(OT_PER)


def _d_counts():
    vsp = 4
    sex = 0
    REL = []
    SS = [[0] * 32 for _ in range(32)]
    for sl in range(32):
        st = SCHED[sl]
        for qt in range(4):
            for ks in range(st["nks"]):
                vsp += 1
                REL.append(("v", vsp))
                vsp += 1
            vsp += 1
        for kc in range(st["nkc"]):
            if kc >= st["band0"]:
                vsp += 1
                REL.append(("v", vsp))
                SS[sl][kc] = vsp
            else:
                sex += 1
                REL.append(("s", sex))
        sex += st["nband"]
        for qt in range(4):
            vsp += 1
    return REL, SS


D_REL, D_SS = _d_counts()

def _d_counts2():
    vsp = 4
    sex = 0
    REL = []
    SS = [[0] * 32 for _ in range(32)]
    for sl in range(32):
        st = SCHED[sl]
        for qt in range(4):
            for ks in range(st["nks"]):
                vsp += 1
                REL.append(("v", vsp))
                vsp += 1
            vsp += 1
        for kc in range(st["nkc"]):
            sex += 1
            if kc >= st["band0"]:
                vsp += 1
                REL.append(("v", vsp))
                SS[sl][kc] = vsp
            else:
                REL.append(("s", sex))
        for qt in range(4):
            vsp += 1
    return REL, SS


D_REL, D_SS = _d_counts2()

GC = [0]
IUB = []
for _sl in range(32):
    _st = SCHED[_sl]
    IUB.append(GC[-1] + 4 * _st["nks"])
    GC.append(GC[-1] + 4 * _st["nks"] + _st["nkc"])

EQ = [0, 22, 44, 66, NIT]
NT_BS = [32, 32, 16]
TILE_BASE = [0, 32, 64]
GRP_BASE = [0, 128, 256]
TOT_GRP = 320


class _NullBlock:
    def __enter__(self):
        return self

    def __exit__(self, *a):
        return False

    def _skip(self, f):
        return None

    gpsimd = scalar = tensor = vector = sync = _skip


def build_nc(stages="ABCDE", dbg=False):
    nc = bass.Bass("TRN2", target_bir_lowering=False)

    xT = nc.dram_tensor("xT", [D, TKV], F32, kind="ExternalInput")
    xqT = nc.dram_tensor("xqT", [D, TQ], F32, kind="ExternalInput")
    xq = nc.dram_tensor("xq", [TQ, D], F32, kind="ExternalInput")
    wqkvP = nc.dram_tensor("wqkvP", [48, 128, DC * 128], F32, kind="ExternalInput")
    woP = nc.dram_tensor("woP", [8, 128, NCT * 512], F32, kind="ExternalInput")
    bqkvT = nc.dram_tensor("bqkvT", [128, 48], F32, kind="ExternalInput")
    bo = nc.dram_tensor("bo", [1, D], F32, kind="ExternalInput")
    wgP = nc.dram_tensor("wgP", [NIT, 128, DC * 128], I8, kind="ExternalInput")
    wuP = nc.dram_tensor("wuP", [NIT, 128, DC * 128], I8, kind="ExternalInput")
    wdP = nc.dram_tensor("wdP", [8, 128, NIT * 512], I8, kind="ExternalInput")
    gs_d = nc.dram_tensor("gs", [128, 1], F32, kind="ExternalInput")
    us_d = nc.dram_tensor("us", [128, 1], F32, kind="ExternalInput")
    ds_d = nc.dram_tensor("ds", [128, 1], F32, kind="ExternalInput")
    off_d = nc.dram_tensor("off", [128, 1], F32, kind="ExternalInput")
    identb_d = nc.dram_tensor("identb", [128, 128], BF16, kind="ExternalInput")
    identf_d = nc.dram_tensor("identf", [128, 128], F32, kind="ExternalInput")
    y = nc.dram_tensor("y", [TQ, D], F32, kind="ExternalOutput")

    KT_sc = nc.dram_tensor("KT_sc", [NCT, 128, TKV], BF16, kind="Internal")
    VT_sc = nc.dram_tensor("VT_sc", [NCT, 128, TKV], BF16, kind="Internal")
    QT_sc = nc.dram_tensor("QT_sc", [NCT, 128, TQ], BF16, kind="Internal")
    P_sc = nc.dram_tensor("P_sc", [16, 128, NIT, 128], BF16, kind="Internal")
    OT_sc = nc.dram_tensor("OT_sc", [16, 128, NCT, 128], BF16, kind="Internal")
    bo_dr = nc.dram_tensor("bo_dr", [1, D], BF16, kind="Internal")

    if dbg:
        dbg_outs = {
            "dKT": nc.dram_tensor("dKT", [NCT, 128, TKV], BF16, kind="ExternalOutput"),
            "dVT": nc.dram_tensor("dVT", [NCT, 128, TKV], BF16, kind="ExternalOutput"),
            "dQT": nc.dram_tensor("dQT", [NCT, 128, TQ], BF16, kind="ExternalOutput"),
            "dP": nc.dram_tensor("dP", [16, 128, NIT, 128], BF16, kind="ExternalOutput"),
            "dOT": nc.dram_tensor("dOT", [16, 128, NCT, 128], BF16, kind="ExternalOutput"),
        }

    with ExitStack() as outer:
        sbo = lambda *a: outer.enter_context(nc.sbuf_tensor(*a))
        semo = lambda n: outer.enter_context(nc.semaphore(n))

        bqkv_s = sbo("bqkv_s", [128, 48], F32)
        gs_s = sbo("gs_s", [128, 1], F32)
        us_s = sbo("us_s", [128, 1], F32)
        ds_s = sbo("ds_s", [128, 1], F32)
        usds = sbo("usds", [128, 1], F32)
        off_s = sbo("off_s", [128, 1], F32)
        identb = sbo("identb_s", [128, 128], BF16)
        identf = sbo("identf_s", [128, 128], F32)
        onesb = sbo("onesb", [128, 1], BF16)
        onesfr = sbo("onesfr", [1, 128], F32)
        onesbr = sbo("onesbr", [1, 128], BF16)
        p_ramp = sbo("p_ramp", [128, 1], F32)
        eps_s = sbo("eps_s", [128, 1], F32)
        negbig_s = sbo("negbig_s", [128, 1], F32)

        dA = semo("dA")
        with ExitStack() as aa:
            bo_f = aa.enter_context(nc.sbuf_tensor("bo_f", [1, D], F32))
            bo_bf = aa.enter_context(nc.sbuf_tensor("bo_bf", [1, D], BF16))
            aA = aa.enter_context(nc.semaphore("aA"))
            aD = aa.enter_context(nc.semaphore("aD"))
            with (nc.Block("setup") if "A" in stages else _NullBlock()) as blk:
                @blk.sync
                def _(s):
                    s.dma_start(bqkv_s[:, :], bqkvT.ap()).then_inc(dA, 16)
                    s.dma_start(gs_s[:, :], gs_d.ap()).then_inc(dA, 16)
                    s.dma_start(us_s[:, :], us_d.ap()).then_inc(dA, 16)
                    s.dma_start(ds_s[:, :], ds_d.ap()).then_inc(dA, 16)
                    s.dma_start(off_s[:, :], off_d.ap()).then_inc(dA, 16)
                    s.dma_start(identb[:, :], identb_d.ap()).then_inc(dA, 16)
                    s.dma_start(identf[:, :], identf_d.ap()).then_inc(dA, 16)
                    s.dma_start(bo_f[:, :], bo.ap()).then_inc(dA, 16)
                    s.wait_ge(dA, 16 * 8)

                @blk.gpsimd
                def _(g):
                    g.memset(onesb[:, :], 1.0)
                    g.memset(eps_s[:, :], EPS)
                    g.memset(negbig_s[:, :], -BIGC)
                    g.memset(onesfr[:, :], 1.0)
                    g.memset(onesbr[:, :], 1.0)
                    g.iota(p_ramp[:, :], [[1, 1]], channel_multiplier=1,
                           allow_small_or_imprecise_dtypes=True)

                @blk.vector
                def _(v):
                    v.wait_ge(dA, 16 * 8)
                    v.tensor_mul(usds[:, :], us_s[:, :], ds_s[:, :])
                    v.tensor_copy(bo_bf[:, :], bo_f[:, :]).then_inc(aA, 1)

                @blk.gpsimd
                def _(g):
                    g.wait_ge(aA, 1)
                    g.dma_start(bo_dr.ap(), bo_bf[:, :]).then_inc(aD, 16)
                    g.wait_ge(aD, 16)

        with ExitStack() as bc:
            xnbig = bc.enter_context(nc.sbuf_tensor("xnbig", [128, DC * 2048], BF16))

            dXs = [semo(f"dX{i}") for i in range(4)]
            dWb = [semo(f"dW{i}") for i in range(4)]
            dSCs = [semo(f"dSC{i}") for i in range(2)]
            sSQ = semo("sSQ")
            sSD = semo("sSD")
            sEV = semo("sEV")
            tSQC = semo("tSQC")
            tREP = semo("tREP")
            tMP = semo("tMP")
            vFIN = semo("vFIN")
            vVAR = semo("vVAR")
            vSTAT = semo("vSTAT")
            vREP = semo("vREP")
            vNORM = semo("vNORM")
            pT1 = semo("pT1")
            vSelfB = semo("vSelfB")

            with ExitStack() as bb:
                sbb = lambda *a: bb.enter_context(nc.sbuf_tensor(*a))
                psb = lambda *a: bb.enter_context(nc.psum_tensor(*a))
                x2t = [sbb(f"x2t{i}", [128, 512], BF16) for i in range(2)]
                t1b = [sbb(f"t1b{i}", [128, 512], F32) for i in range(2)]
                stat = [sbb(f"stat{i}", [1, 512], F32) for i in range(7)]
                r_rep = sbb("r_rep", [128, 512], F32)
                mr_rep = sbb("mr_rep", [128, 512], F32)
                wtile = [sbb(f"wtile{i}", [128, DC * 128], BF16) for i in range(4)]
                ev = [sbb(f"ev{i}", [128, 512], BF16) for i in range(2)]
                ps_sum = psb("ps_sum", [1, 512], F32)
                ps_sq = psb("ps_sq", [1, 512], F32)
                rp0 = psb("rp0", [128, 512], F32)
                rp1 = psb("rp1", [128, 512], F32)
                pp = [psb(f"pp{i}", [128, 512], F32) for i in range(4)]

                def xsrc(bs):
                    return xT if bs < 2 else xqT

                xnv = xnbig[:, :].rearrange("p (a t) -> p a t", t=2048)

                with (nc.Block("b_lnqkv") if "B" in stages else _NullBlock()) as blk:
                    @blk.gpsimd
                    def _(g):
                        for bs in range(3):
                            tiles = _bs_tiles(bs)
                            tok0 = bs * 2048 if bs < 2 else 0
                            srcv = xsrc(bs).ap().rearrange("(a p) t -> p a t", p=128)

                            def xdma(ss_):
                                g.dma_start(
                                    xnv[:, :, ss_ * 512:(ss_ + 1) * 512],
                                    srcv[:, :, tok0 + ss_ * 512:
                                         tok0 + (ss_ + 1) * 512],
                                ).then_inc(dXs[ss_], 16)

                            def wdma(j):
                                Tg = TILE_BASE[bs] + j
                                if Tg >= 4 and PROJ_LIMIT >= TOT_GRP:
                                    g.wait_ge(tMP, 4 * (Tg - 3))
                                g.dma_start(wtile[Tg % 4][:, :],
                                            wqkvP.ap()[tiles[j][2]]
                                            ).then_inc(dWb[Tg % 4], 16)

                            if bs > 0 and PROJ_LIMIT >= TOT_GRP:
                                g.wait_ge(tMP, GRP_BASE[bs])
                            xdma(0)
                            wdma(0)
                            wdma(1)
                            xdma(1)
                            wdma(2)
                            xdma(2)
                            wdma(3)
                            xdma(3)
                            for ss_ in range(4):
                                ssg = bs * 4 + ss_
                                g.wait_ge(vREP, ssg + 1)
                                for dc in range(DC):
                                    k = ssg * 32 + dc
                                    if k >= 2:
                                        g.wait_ge(vNORM, k - 1)
                                    g.tensor_mul(
                                        t1b[k % 2][:, :],
                                        xnbig[:, dc * 2048 + ss_ * 512:
                                              dc * 2048 + (ss_ + 1) * 512],
                                        r_rep[:, :],
                                    ).then_inc(pT1, 1)
                            for j in range(4, NT_BS[bs]):
                                wdma(j)
                        g.wait_ge(pT1, 384)
                        for i in range(4):
                            g.wait_ge(dXs[i], 16 * 3)
                            g.wait_ge(dWb[i], 16 * 20)

                    @blk.tensor
                    def _(t):
                        grp = 0

                        def proj_group(bs, j, nt):
                            nonlocal grp
                            if grp >= PROJ_LIMIT:
                                return
                            Tg = TILE_BASE[bs] + j
                            t.wait_ge(dWb[Tg % 4], 16 * (Tg // 4 + 1))
                            t.wait_ge(vNORM, 32 * (bs * 4 + nt + 1))
                            if grp >= 4:
                                t.wait_ge(sEV, grp - 3)
                            last = None
                            for dc in range(DC):
                                last = t.matmul(
                                    pp[grp % 4][:, :],
                                    wtile[Tg % 4][:, dc * 128:(dc + 1) * 128],
                                    xnbig[:, dc * 2048 + nt * 512:
                                          dc * 2048 + (nt + 1) * 512],
                                    start=(dc == 0), stop=(dc == DC - 1),
                                )
                            last.then_inc(tMP, 1)
                            grp += 1

                        for bs in range(3):
                            for ss_ in range(4):
                                ssg = bs * 4 + ss_
                                for dc in range(DC):
                                    k = ssg * 32 + dc
                                    if dc == 0 and ssg > 0:
                                        t.wait_ge(vFIN, ssg)
                                    if dc == 0:
                                        t.wait_ge(dXs[ss_], 16 * (bs + 1))
                                    t.wait_ge(sSQ, k + 1)
                                    t.matmul(ps_sum[:, :], onesb[:, :],
                                             xnbig[:, dc * 2048 + ss_ * 512:
                                                   dc * 2048 + (ss_ + 1) * 512],
                                             start=(dc == 0), stop=(dc == DC - 1))
                                    t.matmul(ps_sq[:, :], onesb[:, :],
                                             x2t[k % 2][:, :],
                                             start=(dc == 0), stop=(dc == DC - 1),
                                             ).then_inc(tSQC, 1)
                                t.wait_ge(vSTAT, ssg + 1)
                                if ssg > 0:
                                    t.wait_ge(vREP, ssg)
                                t.matmul(rp0[:, :], onesfr[:, :], stat[5][0:1, :],
                                         start=True, stop=True)
                                t.matmul(rp1[:, :], onesfr[:, :], stat[6][0:1, :],
                                         start=True, stop=True).then_inc(tREP, 1)
                                proj_group(bs, 0, ss_)
                            for j in range(1, NT_BS[bs]):
                                for nt in range(4):
                                    proj_group(bs, j, nt)
                        assert grp == min(TOT_GRP, PROJ_LIMIT)

                    @blk.scalar
                    def _(s):
                        grp = 0

                        def evac(j, nt, tiles):
                            nonlocal grp
                            if grp >= PROJ_LIMIT:
                                grp += 1
                                return
                            s.wait_ge(tMP, grp + 1)
                            if grp >= 2:
                                s.wait_ge(dSCs[grp % 2], 16 * ((grp - 2) // 2 + 1))
                            bias_col = tiles[j][2]
                            s.activation(ev[grp % 2][:, :], pp[grp % 4][:, :],
                                         AF.Identity,
                                         bias=bqkv_s[:, bias_col:bias_col + 1],
                                         scale=1.0).then_inc(sEV, 1)
                            grp += 1

                        for bs in range(3):
                            tiles = _bs_tiles(bs)
                            for ss_ in range(4):
                                ssg = bs * 4 + ss_
                                for dc in range(DC):
                                    k = ssg * 32 + dc
                                    if dc == 0:
                                        s.wait_ge(dXs[ss_], 16 * (bs + 1))
                                    if k >= 2:
                                        s.wait_ge(tSQC, k - 1)
                                    s.activation(x2t[k % 2][:, :],
                                                 xnbig[:, dc * 2048 + ss_ * 512:
                                                       dc * 2048 + (ss_ + 1) * 512],
                                                 AF.Square).then_inc(sSQ, 1)
                                s.wait_ge(vVAR, ssg + 1)
                                s.activation(stat[4][0:1, :], stat[3][0:1, :],
                                             AF.Sqrt, bias=eps_s[0:1, 0:1],
                                             scale=1.0).then_inc(sSD, 1)
                                evac(0, ss_, tiles)
                            for j in range(1, NT_BS[bs]):
                                for nt in range(4):
                                    evac(j, nt, tiles)

                    @blk.vector
                    def _(v):
                        vsfB = 0
                        for bs in range(3):
                            for ss_ in range(4):
                                ssg = bs * 4 + ss_
                                v.wait_ge(tSQC, 32 * (ssg + 1))
                                if ssg > 0:
                                    v.wait_ge(sSD, ssg)
                                    v.wait_ge(tREP, ssg)
                                    v.wait_ge(vSTAT, ssg)
                                    v.wait_ge(vVAR, ssg)
                                v.tensor_scalar(stat[0][0:1, :], ps_sum[:, :],
                                                1.0 / D, None, ALU.mult
                                                ).then_inc(vSelfB, 1)
                                vsfB += 1
                                v.tensor_scalar(stat[1][0:1, :], ps_sq[:, :],
                                                1.0 / D, None, ALU.mult
                                                ).then_inc(vFIN, 1)
                                v.wait_ge(vSelfB, vsfB)
                                v.wait_ge(vFIN, ssg + 1)
                                v.tensor_mul(stat[2][0:1, :], stat[0][0:1, :],
                                             stat[0][0:1, :]).then_inc(vSelfB, 1)
                                vsfB += 1
                                v.wait_ge(vSelfB, vsfB)
                                v.tensor_sub(stat[3][0:1, :], stat[1][0:1, :],
                                             stat[2][0:1, :]).then_inc(vVAR, 1)
                                v.wait_ge(sSD, ssg + 1)
                                v.reciprocal(stat[5][0:1, :], stat[4][0:1, :]
                                             ).then_inc(vSelfB, 1)
                                vsfB += 1
                                v.wait_ge(vSelfB, vsfB)
                                v.tensor_mul(stat[6][0:1, :], stat[0][0:1, :],
                                             stat[5][0:1, :]).then_inc(vSTAT, 1)
                                v.wait_ge(tREP, ssg + 1)
                                if ssg > 0:
                                    v.wait_ge(pT1, 32 * ssg)
                                v.tensor_copy(r_rep[:, :], rp0[:, :])
                                v.tensor_copy(mr_rep[:, :], rp1[:, :]
                                              ).then_inc(vREP, 1)
                                v.wait_ge(vREP, ssg + 1)
                                for dc in range(DC):
                                    k = ssg * 32 + dc
                                    v.wait_ge(pT1, k + 1)
                                    v.tensor_sub(
                                        xnbig[:, dc * 2048 + ss_ * 512:
                                              dc * 2048 + (ss_ + 1) * 512],
                                        t1b[k % 2][:, :], mr_rep[:, :]
                                    ).then_inc(vNORM, 1)

                    @blk.sync
                    def _(s):
                        grp = 0
                        for bs in range(3):
                            tiles = _bs_tiles(bs)
                            tok0 = bs * 2048 if bs < 2 else 0
                            order = [(0, nt) for nt in range(4)] + [
                                (j, nt) for j in range(1, NT_BS[bs])
                                for nt in range(4)]
                            for (j, nt) in order:
                                if grp >= PROJ_LIMIT:
                                    grp += 1
                                    continue
                                s.wait_ge(sEV, grp + 1)
                                kind, ct, _ = tiles[j]
                                dst = {"K": KT_sc, "V": VT_sc, "Q": QT_sc}[kind]
                                c0 = (tok0 if kind != "Q" else 0) + nt * 512
                                s.dma_start(dst.ap()[ct][:, c0:c0 + 512],
                                            ev[grp % 2][:, :]
                                            ).then_inc(dSCs[grp % 2], 16)
                                grp += 1
                        s.wait_ge(dSCs[0], 16 * 160)
                        s.wait_ge(dSCs[1], 16 * 160)

            dWGs = [semo(f"dWG{i}") for i in range(2)]
            dWUs = [semo(f"dWU{i}") for i in range(2)]
            dPs = [semo(f"dP{i}") for i in range(3)]
            tG = semo("tG")
            tU = semo("tU")
            sSI = semo("sSI")
            vUT = semo("vUT")
            vPT = semo("vPT")
            NG = NIT * 4

            with ExitStack() as ccx:
                sbc = lambda *a: ccx.enter_context(nc.sbuf_tensor(*a))
                psc = lambda *a: ccx.enter_context(nc.psum_tensor(*a))
                wg_bf = [sbc(f"wg_bf{i}", [128, DC * 128], BF16) for i in range(2)]
                wu_bf = [sbc(f"wu_bf{i}", [128, DC * 128], BF16) for i in range(2)]
                gt = [sbc(f"gt{i}", [128, 512], BF16) for i in range(3)]
                ut = [sbc(f"ut{i}", [128, 512], BF16) for i in range(3)]
                ptile = [sbc(f"ptile{i}", [128, 512], BF16) for i in range(3)]
                g_ps = [psc(f"g_ps{i}", [128, 512], F32) for i in range(3)]
                u_ps = [psc(f"u_ps{i}", [128, 512], F32) for i in range(3)]

                with (nc.Block("c_ffn") if "C" in stages else _NullBlock()) as blk:
                    @blk.gpsimd
                    def _(g):
                        for it in range(NIT):
                            if it >= 2:
                                g.wait_ge(tU, 4 * (it - 1))
                            g.dma_start(wg_bf[it % 2][:, :], wgP.ap()[it]
                                        ).then_inc(dWGs[it % 2], 16)
                            g.dma_start(wu_bf[it % 2][:, :], wuP.ap()[it]
                                        ).then_inc(dWUs[it % 2], 16)
                        g.wait_ge(dWGs[0], 16 * 43)
                        g.wait_ge(dWGs[1], 16 * 43)
                        g.wait_ge(dWUs[0], 16 * 43)
                        g.wait_ge(dWUs[1], 16 * 43)

                    @blk.tensor
                    def _(t):
                        for it in range(NIT):
                            t.wait_ge(dWGs[it % 2], 16 * (it // 2 + 1))
                            t.wait_ge(dWUs[it % 2], 16 * (it // 2 + 1))
                            for nt in range(4):
                                gidx = it * 4 + nt
                                if gidx >= 3:
                                    t.wait_ge(sSI, gidx - 2)
                                last = None
                                for dc in range(DC):
                                    last = t.matmul(
                                        g_ps[gidx % 3][:, :],
                                        wg_bf[it % 2][:, dc * 128:(dc + 1) * 128],
                                        xnbig[:, dc * 2048 + nt * 512:
                                              dc * 2048 + (nt + 1) * 512],
                                        start=(dc == 0), stop=(dc == DC - 1))
                                last.then_inc(tG, 1)
                                if gidx >= 3:
                                    t.wait_ge(vUT, gidx - 2)
                                for dc in range(DC):
                                    last = t.matmul(
                                        u_ps[gidx % 3][:, :],
                                        wu_bf[it % 2][:, dc * 128:(dc + 1) * 128],
                                        xnbig[:, dc * 2048 + nt * 512:
                                              dc * 2048 + (nt + 1) * 512],
                                        start=(dc == 0), stop=(dc == DC - 1))
                                last.then_inc(tU, 1)

                    @blk.scalar
                    def _(s):
                        for gidx in range(NG):
                            s.wait_ge(tG, gidx + 1)
                            if gidx >= 3:
                                s.wait_ge(vPT, gidx - 2)
                            s.activation(gt[gidx % 3][:, :], g_ps[gidx % 3][:, :],
                                         AF.Silu, bias=0.0, scale=gs_s[:, 0:1]
                                         ).then_inc(sSI, 1)

                    @blk.vector
                    def _(v):
                        for gidx in range(NG):
                            v.wait_ge(tU, gidx + 1)
                            if gidx >= 3:
                                v.wait_ge(vPT, gidx - 2)
                            v.tensor_scalar(ut[gidx % 3][:, :],
                                            u_ps[gidx % 3][:, :],
                                            usds[:, 0:1], None, ALU.mult
                                            ).then_inc(vUT, 1)
                            v.wait_ge(vUT, gidx + 1)
                            v.wait_ge(sSI, gidx + 1)
                            if gidx >= 3:
                                v.wait_ge(dPs[gidx % 3], 16 * ((gidx - 3) // 3 + 1))
                            v.tensor_mul(ptile[gidx % 3][:, :], gt[gidx % 3][:, :],
                                         ut[gidx % 3][:, :]).then_inc(vPT, 1)

                    @blk.sync
                    def _(s):
                        for it in range(NIT):
                            for nt in range(4):
                                gidx = it * 4 + nt
                                s.wait_ge(vPT, gidx + 1)
                                dst = bass.AP(
                                    P_sc, (nt * 4) * 128 * NIT * 128 + it * 128,
                                    [[NIT * 128, 128], [128 * NIT * 128, 4],
                                     [1, 128]])
                                s.dma_start(dst,
                                            ptile[gidx % 3][:, :].rearrange(
                                                "p (a u) -> p a u", u=128)
                                            ).then_inc(dPs[gidx % 3], 16)
                        s.wait_ge(dPs[0], 16 * 115)
                        s.wait_ge(dPs[1], 16 * 115)
                        s.wait_ge(dPs[2], 16 * 114)

        dQs = [semo(f"dQ{i}") for i in range(2)]
        dKs = [semo(f"dK{i}") for i in range(2)]
        dVs = [semo(f"dV{i}") for i in range(2)]
        dOTs = [semo(f"dOT{i}") for i in range(2)]
        gMS = semo("gMS")
        tS = semo("tS")
        tMT = semo("tMT")
        tRP = semo("tRP")
        tAV = semo("tAV")
        tTR = semo("tTR")
        sMR = semo("sMR")
        sEX = semo("sEX")
        sOT = semo("sOT")
        vSP = semo("vSP")
        vM4 = semo("vM4")
        vRP = semo("vRP")
        vO = semo("vO")

        with ExitStack() as dd:
            sbd = lambda *a: dd.enter_context(nc.sbuf_tensor(*a))
            psd = lambda *a: dd.enter_context(nc.psum_tensor(*a))
            qt_buf = [sbd(f"qt_buf{i}", [128, 2 * TQ], BF16) for i in range(2)]
            kt_buf = [sbd(f"kt_buf{i}", [128, 2 * TKV], BF16) for i in range(2)]
            vaug = [sbd(f"vaug{i}", [128, 32 * VW], BF16) for i in range(2)]
            ptil = sbd("ptil", [128, 32 * 512], BF16)
            km_tab = sbd("km_tab", [128, 3072], BF16)
            qm_tab = sbd("qm_tab", [128, 6144], BF16)
            itmp = sbd("itmp", [128, 6144], F32)
            stat_tmp = [sbd(f"stat_tmp{i}", [128, 512], F32) for i in range(2)]
            ssubB = [sbd(f"ssubB{i}", [128, 512], F32) for i in range(2)]
            mcols = sbd("mcols", [128, 8], F32)
            m4 = sbd("m4", [128, 4], F32)
            nm_bf = sbd("nm_bf", [1, 512], BF16)
            thr2 = sbd("thr2", [128, 1], F32)
            rl = [sbd(f"rl{i}", [128, 1], F32) for i in range(2)]
            o_sb = [sbd(f"o_sb{i}", [128, 256], BF16) for i in range(2)]
            ot_stg = [sbd(f"ot_stg{i}", [128, 128], BF16) for i in range(2)]
            s_ps = [psd(f"s_ps{i}", [128, 512], F32) for i in range(3)]
            o_ps = [psd(f"o_ps{i}", [128, 257], F32) for i in range(2)]
            mrow_ps = psd("mrow_ps", [1, 512], F32)
            tr_ps = [psd(f"tr_ps{i}", [128, 1024], BF16) for i in range(2)]

            with (nc.Block("d_attn") if "D" in stages else _NullBlock()) as blk:
                @blk.gpsimd
                def _(g):
                    g.iota(itmp[:, :], [[1, 6144]], base=-1920,
                           channel_multiplier=0,
                           allow_small_or_imprecise_dtypes=True
                           ).then_inc(gMS, 1)
                    g.wait_ge(vSP, 2)
                    g.iota(itmp[:, 0:3072], [[1, 3072]], base=-2432,
                           channel_multiplier=0,
                           allow_small_or_imprecise_dtypes=True
                           ).then_inc(gMS, 1)
                    g.memset(vaug[0][:, :], 0.0).then_inc(gMS, 1)
                    g.memset(vaug[1][:, :], 0.0).then_inc(gMS, 1)
                    g.wait_ge(gMS, 4)
                    g.memset(vaug[0][:, :].rearrange(
                        "p (a u) -> p a u", u=VW)[:, :, 256:257], 1.0
                        ).then_inc(gMS, 1)
                    g.memset(vaug[1][:, :].rearrange(
                        "p (a u) -> p a u", u=VW)[:, :, 256:257], 1.0
                        ).then_inc(gMS, 1)

                @blk.sync
                def _(s):
                    for hh in range(8):
                        if hh > 0:
                            s.wait_ge(tAV, AV_CUM[4 * (hh - 1)])
                        s.dma_start(
                            qt_buf[hh % 2][:, :]
                            .rearrange("p (a t) -> p a t", t=TQ),
                            QT_sc.ap()[2 * hh:2 * hh + 2]
                            .rearrange("a p t -> p a t"),
                        ).then_inc(dQs[hh % 2], 16)
                        s.dma_start(
                            kt_buf[hh % 2][:, :]
                            .rearrange("p (a t) -> p a t", t=TKV),
                            KT_sc.ap()[2 * hh:2 * hh + 2]
                            .rearrange("a p t -> p a t"),
                        ).then_inc(dKs[hh % 2], 16)
                    for i in range(2):
                        s.wait_ge(dQs[i], 16 * 4)
                        s.wait_ge(dKs[i], 16 * 4)

                @blk.scalar
                def _(s):
                    otk = 0
                    for hh in range(8):
                        if hh == 0:
                            s.wait_ge(gMS, 6)
                        else:
                            s.wait_ge(tAV, AV_CUM[4 * (hh - 1)])
                        s.dma_start(
                            vaug[hh % 2][:, :].rearrange(
                                "p (a u) -> p a u", u=VW)[:, :, 0:256],
                            VT_sc.ap()[2 * hh:2 * hh + 2]
                            .rearrange("a p t -> (a p) t"),
                            transpose=True,
                        ).then_inc(dVs[hh % 2], 16)
                        for qs in range(4):
                            sl = hh * 4 + qs
                            st = SCHED[sl]
                            s.wait_ge(tMT, sl + 1)
                            if sl > 0:
                                s.wait_ge(tS, GC[sl])
                            s.copy(nm_bf[0:1, :], mrow_ps[:, :]).then_inc(sMR, 1)
                            for kc in range(st["nkc"]):
                                if sl > 0 and kc == 0:
                                    s.wait_ge(tAV, AV_CUM[sl])
                                if kc >= st["band0"]:
                                    s.wait_ge(vSP, D_SS[sl][kc])
                                    s.activation(ptil[:, kc * 512:(kc + 1) * 512],
                                                 ssubB[kc % 2][:, :], AF.Exp,
                                                 bias=negbig_s[:, 0:1]
                                                 ).then_inc(sEX, 1)
                                else:
                                    iu_kc = IUB[sl] + kc
                                    s.wait_ge(tS, iu_kc + 1)
                                    s.activation(ptil[:, kc * 512:(kc + 1) * 512],
                                                 s_ps[iu_kc % 3][:, :], AF.Exp
                                                 ).then_inc(sEX, 1)
                            for qt in range(4):
                                for j2 in range(2):
                                    s.wait_ge(tTR, OT_CUM[sl] + qt * 2 + j2 + 1)
                                    if otk >= 2:
                                        s.wait_ge(dOTs[otk % 2],
                                                  16 * ((otk - 2) // 2 + 1))
                                    s.copy(ot_stg[otk % 2][:, :],
                                           tr_ps[otk % 2][:, 0:128]
                                           ).then_inc(sOT, 1)
                                    s.wait_ge(sOT, otk + 1)
                                    cc = 2 * hh + j2
                                    tt = qs * 4 + qt
                                    s.dma_start(
                                        bass.AP(OT_sc,
                                                tt * 128 * NCT * 128 + cc * 128,
                                                [[NCT * 128, 128], [1, 128]]),
                                        ot_stg[otk % 2][:, :]
                                        ).then_inc(dOTs[otk % 2], 16)
                                    otk += 1
                    s.wait_ge(dOTs[0], 16 * 128)
                    s.wait_ge(dOTs[1], 16 * 128)
                    s.wait_ge(dVs[0], 16 * 4)
                    s.wait_ge(dVs[1], 16 * 4)

                @blk.tensor
                def _(t):
                    iu = 0
                    gO = 0
                    ktr = 0
                    for hh in range(8):
                        t.wait_ge(dQs[hh % 2], 16 * (hh // 2 + 1))
                        t.wait_ge(dKs[hh % 2], 16 * (hh // 2 + 1))
                        for qs in range(4):
                            sl = hh * 4 + qs
                            st = SCHED[sl]
                            for qt in range(4):
                                for ks in range(st["nks"]):
                                    if iu >= 3:
                                        k, c = D_REL[iu - 3]
                                        t.wait_ge(vSP if k == "v" else sEX, c)
                                    last = None
                                    for j in range(2):
                                        last = t.matmul(
                                            s_ps[iu % 3][:, :],
                                            qt_buf[hh % 2][:,
                                                           j * TQ + qs * 512
                                                           + qt * 128:
                                                           j * TQ + qs * 512
                                                           + (qt + 1) * 128],
                                            kt_buf[hh % 2][:,
                                                           j * TKV + ks * 512:
                                                           j * TKV
                                                           + (ks + 1) * 512],
                                            start=(j == 0), stop=(j == 1))
                                    last.then_inc(tS, 1)
                                    iu += 1
                            t.wait_ge(vM4, 4 * (sl + 1))
                            if sl > 0:
                                t.wait_ge(sMR, sl)
                            last = None
                            for qt in range(4):
                                last = t.transpose(
                                    mrow_ps[0:1, qt * 128:(qt + 1) * 128],
                                    m4[:, qt:qt + 1], identf[:, :])
                            last.then_inc(tMT, 1)
                            t.wait_ge(sMR, sl + 1)
                            for kc in range(st["nkc"]):
                                if iu >= 3:
                                    k, c = D_REL[iu - 3]
                                    t.wait_ge(vSP if k == "v" else sEX, c)
                                for j in range(2):
                                    t.matmul(
                                        s_ps[iu % 3][:, :],
                                        kt_buf[hh % 2][:, j * TKV + kc * 128:
                                                       j * TKV + (kc + 1) * 128],
                                        qt_buf[hh % 2][:, j * TQ + qs * 512:
                                                       j * TQ + (qs + 1) * 512],
                                        start=(j == 0), stop=False)
                                t.matmul(s_ps[iu % 3][:, :], onesbr[:, :],
                                         nm_bf[0:1, :], start=False, stop=True
                                         ).then_inc(tS, 1)
                                iu += 1
                            t.wait_ge(sEX, EX_CUM[sl + 1])
                            t.wait_ge(dVs[hh % 2], 16 * (hh // 2 + 1))
                            for qt in range(4):
                                if gO >= 2:
                                    t.wait_ge(vO, gO - 1)
                                last = None
                                for kc in range(st["nav"][qt]):
                                    last = t.matmul(
                                        o_ps[gO % 2][:, :],
                                        ptil[:, kc * 512 + qt * 128:
                                             kc * 512 + (qt + 1) * 128],
                                        vaug[hh % 2][:, kc * VW: kc * VW + 257],
                                        start=(kc == 0),
                                        stop=(kc == st["nav"][qt] - 1))
                                last.then_inc(tAV, 1)
                                gO += 1
                                t.wait_ge(vO, gO)
                                for j2 in range(2):
                                    if ktr >= 2:
                                        t.wait_ge(sOT, ktr - 1)
                                    t.transpose(tr_ps[ktr % 2][:, 0:128],
                                                o_sb[qt % 2][:, j2 * 128:
                                                             (j2 + 1) * 128],
                                                identb[:, :]).then_inc(tTR, 1)
                                    ktr += 1

                @blk.vector
                def _(v):
                    iu = 0
                    gO = 0
                    vsp = 0
                    v.wait_ge(gMS, 1)
                    v.tensor_scalar(thr2[:, :], p_ramp[:, :], 0.0,
                                    off_s[:, 0:1], ALU.add, ALU.add
                                    ).then_inc(vSP, 1)
                    vsp += 1
                    v.wait_ge(vSP, vsp)
                    v.tensor_scalar(qm_tab[:, :], itmp[:, :], thr2[:, 0:1],
                                    None, ALU.is_le).then_inc(vSP, 1)
                    vsp += 1
                    v.wait_ge(vSP, vsp)
                    v.wait_ge(gMS, 2)
                    v.tensor_scalar(thr2[:, :], p_ramp[:, :], 0.0,
                                    off_s[:, 0:1], ALU.add, ALU.subtract
                                    ).then_inc(vSP, 1)
                    vsp += 1
                    v.wait_ge(vSP, vsp)
                    v.tensor_scalar(km_tab[:, :], itmp[:, 0:3072], thr2[:, 0:1],
                                    None, ALU.is_ge).then_inc(vSP, 1)
                    vsp += 1
                    v.wait_ge(vSP, vsp)
                    assert vsp == 4
                    for hh in range(8):
                        for qs in range(4):
                            sl = hh * 4 + qs
                            st = SCHED[sl]
                            for qt in range(4):
                                tt16 = qs * 4 + qt
                                for ks in range(st["nks"]):
                                    qm0 = 1920 + ks * 512 - tt16 * 128
                                    v.wait_ge(tS, iu + 1)
                                    if vsp >= 3:
                                        v.wait_ge(vSP, vsp - 2)
                                    v.scalar_tensor_tensor(
                                        stat_tmp[ks % 2][:, :],
                                        s_ps[iu % 3][:, :],
                                        BIGC,
                                        qm_tab[:, qm0:qm0 + 512],
                                        ALU.add, ALU.mult).then_inc(vSP, 1)
                                    vsp += 1
                                    assert D_REL[iu] == ("v", vsp)
                                    iu += 1
                                    v.wait_ge(vSP, vsp)
                                    v.tensor_reduce(mcols[:, ks:ks + 1],
                                                    stat_tmp[ks % 2][:, :],
                                                    AX.X, ALU.max).then_inc(vSP, 1)
                                    vsp += 1
                                v.wait_ge(vSP, vsp)
                                if sl > 0 and qt == 0:
                                    v.wait_ge(tMT, sl)
                                v.tensor_reduce(m4[:, qt:qt + 1],
                                                mcols[:, 0:st["nks"]],
                                                AX.X, ALU.max, negate=True
                                                ).then_inc(vSP, 1)
                                vsp += 1
                                v.wait_ge(vSP, vsp)
                                v.tensor_scalar(m4[:, qt:qt + 1],
                                                m4[:, qt:qt + 1], BIGC, None,
                                                ALU.add).then_inc(vM4, 1)
                            for kc in range(st["nkc"]):
                                if kc >= st["band0"]:
                                    bi = kc - st["band0"]
                                    km0 = 2432 - bi * 128
                                    v.wait_ge(tS, iu + 1)
                                    if kc >= 2:
                                        v.wait_ge(sEX, EX_CUM[sl] + kc - 1)
                                    elif sl > 0:
                                        v.wait_ge(sEX, EX_CUM[sl])
                                    v.scalar_tensor_tensor(
                                        ssubB[kc % 2][:, :],
                                        s_ps[iu % 3][:, :], BIGC,
                                        km_tab[:, km0:km0 + 512],
                                        ALU.add, ALU.mult).then_inc(vSP, 1)
                                    vsp += 1
                                    assert vsp == D_SS[sl][kc]
                                    assert D_REL[iu] == ("v", vsp)
                                iu += 1
                            for qt in range(4):
                                v.wait_ge(tAV, gO + 1)
                                if gO >= 2:
                                    v.wait_ge(tTR, 2 * (gO - 1))
                                if gO >= 2:
                                    v.wait_ge(vO, gO - 1)
                                v.reciprocal(rl[gO % 2][:, :],
                                             o_ps[gO % 2][:, 256:257]
                                             ).then_inc(vSP, 1)
                                vsp += 1
                                v.wait_ge(vSP, vsp)
                                v.tensor_scalar(o_sb[qt % 2][:, :],
                                                o_ps[gO % 2][:, 0:256],
                                                rl[gO % 2][:, 0:1], None,
                                                ALU.mult).then_inc(vO, 1)
                                gO += 1

        dWDs = [semo(f"dWD{i}") for i in range(3)]
        dWOx = semo("dWOx")
        dBO = semo("dBO")
        dPPs = [semo(f"dPP{i}") for i in range(3)]
        dOPs = [semo(f"dOP{i}") for i in range(2)]
        dXQs = [semo(f"dXQ{i}") for i in range(2)]
        dYs = [semo(f"dY{i}") for i in range(2)]
        tE = semo("tE")
        vE = semo("vE")

        with ExitStack() as ee:
            sbe = lambda *a: ee.enter_context(nc.sbuf_tensor(*a))
            pse = lambda *a: ee.enter_context(nc.psum_tensor(*a))
            wdh = [sbe(f"wdh{i}", [128, 43 * 512], BF16) for i in range(3)]
            wo_bf = sbe("wo_bf", [128, NCT * 512], BF16)
            bo_sl = sbe("bo_sl", [1, 512], BF16)
            p_pan = [sbe(f"p_pan{i}", [128, 43 * 128], BF16) for i in range(3)]
            ot_pan = [sbe(f"ot_pan{i}", [128, NCT * 128], BF16) for i in range(2)]
            xq_t = [sbe(f"xq_t{i}", [128, 512], F32) for i in range(2)]
            out_sb = [sbe(f"out_sb{i}", [128, 512], F32) for i in range(2)]
            out_ps = [pse(f"out_ps{i}", [128, 512], F32) for i in range(3)]

            with (nc.Block("e_out") if "E" in stages else _NullBlock()) as blk:
                @blk.gpsimd
                def _(g):
                    for dsl in range(8):
                        if dsl > 0:
                            g.wait_ge(tE, 16 * (dsl - 1) + 1)
                        h2 = 2 * dsl
                        g.dma_start(wdh[h2 % 3][:, :],
                                    wdP.ap()[dsl][:, 0:43 * 512]
                                    ).then_inc(dWDs[h2 % 3], 16)
                        if dsl > 0:
                            g.wait_ge(tE, 16 * dsl)
                        h2 = 2 * dsl + 1
                        g.dma_start(wdh[h2 % 3][:, :],
                                    wdP.ap()[dsl][:, 43 * 512:86 * 512]
                                    ).then_inc(dWDs[h2 % 3], 16)
                        g.dma_start(wo_bf[:, :], woP.ap()[dsl]).then_inc(dWOx, 16)
                        g.dma_start(bo_sl[:, :],
                                    bo_dr.ap()[:, dsl * 512:(dsl + 1) * 512]
                                    ).then_inc(dBO, 16)
                    g.wait_ge(dWDs[0], 16 * 6)
                    g.wait_ge(dWDs[1], 16 * 5)
                    g.wait_ge(dWDs[2], 16 * 5)
                    g.wait_ge(dWOx, 16 * 8)
                    g.wait_ge(dBO, 16 * 8)

                @blk.sync
                def _(s):
                    for gi in range(128):
                        dsl, tt = gi // 16, gi % 16
                        src_flat = P_sc.ap()[tt].rearrange("p a u -> p (a u)")
                        if gi >= 2:
                            s.wait_ge(tE, gi - 1)
                        s.dma_start(p_pan[(2 * gi) % 3][:, :],
                                    src_flat[:, 0:43 * 128]
                                    ).then_inc(dPPs[(2 * gi) % 3], 16)
                        if gi >= 1:
                            s.wait_ge(tE, gi)
                        s.dma_start(p_pan[(2 * gi + 1) % 3][:, :],
                                    src_flat[:, 43 * 128:NIT * 128]
                                    ).then_inc(dPPs[(2 * gi + 1) % 3], 16)
                        if gi >= 1:
                            pg = gi - 1
                            pdsl, ptt = pg // 16, pg % 16
                            s.wait_ge(vE, pg + 1)
                            s.dma_start(
                                y.ap()[ptt * 128:(ptt + 1) * 128,
                                       pdsl * 512:(pdsl + 1) * 512],
                                out_sb[pg % 2][:, :]).then_inc(dYs[pg % 2], 16)
                    s.wait_ge(vE, 128)
                    s.dma_start(y.ap()[15 * 128:16 * 128, 7 * 512:8 * 512],
                                out_sb[127 % 2][:, :]).then_inc(dYs[1], 16)
                    for i in range(3):
                        s.wait_ge(dPPs[i], 16 * (86 if i == 0 else 85))
                    s.wait_ge(dYs[0], 16 * 64)
                    s.wait_ge(dYs[1], 16 * 64)

                @blk.scalar
                def _(s):
                    for gi in range(128):
                        dsl, tt = gi // 16, gi % 16
                        if gi >= 2:
                            s.wait_ge(tE, gi - 1)
                        s.dma_start(
                            ot_pan[gi % 2][:, :],
                            OT_sc.ap()[tt].rearrange("p a u -> p (a u)"),
                        ).then_inc(dOPs[gi % 2], 16)
                        if gi >= 2:
                            s.wait_ge(vE, gi - 1)
                        s.dma_start(
                            xq_t[gi % 2][:, :],
                            xq.ap()[tt * 128:(tt + 1) * 128,
                                    dsl * 512:(dsl + 1) * 512],
                        ).then_inc(dXQs[gi % 2], 16)
                    for i in range(2):
                        s.wait_ge(dOPs[i], 16 * 64)
                        s.wait_ge(dXQs[i], 16 * 64)

                @blk.tensor
                def _(t):
                    for gi in range(128):
                        dsl = gi // 16
                        t.wait_ge(dOPs[gi % 2], 16 * (gi // 2 + 1))
                        if gi >= 3:
                            t.wait_ge(vE, gi - 2)
                        ha = wdh[(2 * dsl) % 3]
                        hb = wdh[(2 * dsl + 1) % 3]
                        pa = p_pan[(2 * gi) % 3]
                        pb = p_pan[(2 * gi + 1) % 3]
                        for ic in range(NIT):
                            if ic == 0:
                                t.wait_ge(dWDs[(2 * dsl) % 3],
                                          16 * ((2 * dsl) // 3 + 1))
                                t.wait_ge(dPPs[(2 * gi) % 3],
                                          16 * ((2 * gi) // 3 + 1))
                            elif ic == 43:
                                t.wait_ge(dWDs[(2 * dsl + 1) % 3],
                                          16 * ((2 * dsl + 1) // 3 + 1))
                                t.wait_ge(dPPs[(2 * gi + 1) % 3],
                                          16 * ((2 * gi + 1) // 3 + 1))
                            wsl = (ha[:, ic * 512:(ic + 1) * 512] if ic < 43
                                   else hb[:, (ic - 43) * 512:(ic - 42) * 512])
                            psl = (pa[:, ic * 128:(ic + 1) * 128] if ic < 43
                                   else pb[:, (ic - 43) * 128:(ic - 42) * 128])
                            t.matmul(out_ps[gi % 3][:, :], psl,
                                     wsl, start=(ic == 0), stop=False)
                        t.wait_ge(dWOx, 16 * (dsl + 1))
                        t.wait_ge(dBO, 16 * (dsl + 1))
                        for cc in range(NCT):
                            t.matmul(out_ps[gi % 3][:, :],
                                     ot_pan[gi % 2][:, cc * 128:(cc + 1) * 128],
                                     wo_bf[:, cc * 512:(cc + 1) * 512],
                                     start=False, stop=False)
                        t.matmul(out_ps[gi % 3][:, :], onesbr[:, :],
                                 bo_sl[0:1, :],
                                 start=False, stop=True).then_inc(tE, 1)

                @blk.vector
                def _(v):
                    for gi in range(128):
                        v.wait_ge(tE, gi + 1)
                        v.wait_ge(dXQs[gi % 2], 16 * (gi // 2 + 1))
                        if gi >= 2:
                            v.wait_ge(dYs[gi % 2], 16 * ((gi - 2) // 2 + 1))
                        v.tensor_add(out_sb[gi % 2][:, :], out_ps[gi % 3][:, :],
                                     xq_t[gi % 2][:, :]).then_inc(vE, 1)

    if dbg:
        with nc.semaphore("dDBG") as dDBG, nc.Block("dbgcopy") as blk:
            @blk.sync
            def _(s):
                for name, srct in [("dKT", KT_sc), ("dVT", VT_sc), ("dQT", QT_sc),
                                   ("dP", P_sc), ("dOT", OT_sc)]:
                    s.dma_start(dbg_outs[name].ap(), srct.ap()).then_inc(dDBG, 16)
                s.wait_ge(dDBG, 16 * 5)
    return nc


_NC_CACHE = {}


def _get_nc():
    if "nc" not in _NC_CACHE:
        _NC_CACHE["nc"] = build_nc()
    return _NC_CACHE["nc"]


def _pack_int8(w):
    w = np.ascontiguousarray(np.asarray(w))
    assert w.dtype == np.int32
    return w.view(np.int8)[:, 0::4]


def host_inputs(L0_attn_x, L0_attn_w_qkv, L0_attn_b_qkv, L0_attn_w_o,
                L0_attn_b_o, L0_ffn_gate_w, L0_ffn_gate_s, L0_ffn_up_w,
                L0_ffn_up_s, L0_ffn_down_w, L0_ffn_down_s):
    x = np.asarray(L0_attn_x, dtype=np.float32)
    wqkvT = np.ascontiguousarray(np.asarray(L0_attn_w_qkv, dtype=np.float32).T)
    wqkvP = np.ascontiguousarray(
        wqkvT.reshape(DC, 128, 48, 128).transpose(2, 1, 0, 3)
        .reshape(48, 128, DC * 128))
    woT = np.ascontiguousarray(np.asarray(L0_attn_w_o, dtype=np.float32).T)
    woP = np.ascontiguousarray(
        woT.reshape(NCT, 128, 8, 512).transpose(2, 1, 0, 3)
        .reshape(8, 128, NCT * 512))
    bqkvT = np.ascontiguousarray(
        np.asarray(L0_attn_b_qkv, dtype=np.float32).reshape(48, 128).T)
    bo_h = np.ascontiguousarray(
        np.asarray(L0_attn_b_o, dtype=np.float32).reshape(1, D))
    wgT8 = np.ascontiguousarray(_pack_int8(L0_ffn_gate_w).T)
    wgP = np.ascontiguousarray(
        wgT8.reshape(DC, 128, NIT, 128).transpose(2, 1, 0, 3)
        .reshape(NIT, 128, DC * 128))
    wuT8 = np.ascontiguousarray(_pack_int8(L0_ffn_up_w).T)
    wuP = np.ascontiguousarray(
        wuT8.reshape(DC, 128, NIT, 128).transpose(2, 1, 0, 3)
        .reshape(NIT, 128, DC * 128))
    wdT8 = np.ascontiguousarray(_pack_int8(L0_ffn_down_w).T)
    wdP = np.ascontiguousarray(
        wdT8.reshape(NIT, 128, 8, 512).transpose(2, 1, 0, 3)
        .reshape(8, 128, NIT * 512))
    gs = np.ascontiguousarray(np.broadcast_to(
        np.asarray(L0_ffn_gate_s, dtype=np.float32).reshape(1, 1), (128, 1)))
    us = np.ascontiguousarray(np.broadcast_to(
        np.asarray(L0_ffn_up_s, dtype=np.float32).reshape(1, 1), (128, 1)))
    ds = np.ascontiguousarray(np.broadcast_to(
        np.asarray(L0_ffn_down_s, dtype=np.float32).reshape(1, 1), (128, 1)))
    identb = np.eye(128, dtype=ml_dtypes.bfloat16)
    identf = np.eye(128, dtype=np.float32)

    xTb = [np.ascontiguousarray(x[b].T) for b in range(4)]
    in_maps = []
    for c in range(8):
        b, h = c // 2, c % 2
        in_maps.append(dict(
            xT=xTb[b],
            xqT=np.ascontiguousarray(x[b, h * TQ:(h + 1) * TQ].T),
            xq=np.ascontiguousarray(x[b, h * TQ:(h + 1) * TQ]),
            wqkvP=wqkvP, woP=woP, bqkvT=bqkvT, bo=bo_h,
            wgP=wgP, wuP=wuP, wdP=wdP,
            gs=gs, us=us, ds=ds,
            off=np.full((128, 1), float(h * TQ), np.float32),
            identb=identb, identf=identf,
        ))
    return in_maps


def kernel(L0_attn_x, L0_attn_w_qkv, L0_attn_b_qkv, L0_attn_w_o, L0_attn_b_o,
           L0_ffn_gate_w, L0_ffn_gate_s, L0_ffn_up_w, L0_ffn_up_s,
           L0_ffn_down_w, L0_ffn_down_s):
    in_maps = host_inputs(L0_attn_x, L0_attn_w_qkv, L0_attn_b_qkv, L0_attn_w_o,
                          L0_attn_b_o, L0_ffn_gate_w, L0_ffn_gate_s,
                          L0_ffn_up_w, L0_ffn_up_s, L0_ffn_down_w,
                          L0_ffn_down_s)
    nc = _get_nc()
    res = run_bass_kernel_spmd(nc, in_maps, core_ids=list(range(8)))
    out = np.empty((4, 4096, D), dtype=np.float32)
    for c in range(8):
        b, h = c // 2, c % 2
        out[b, h * TQ:(h + 1) * TQ] = np.asarray(res.results[c]["y"],
                                                 dtype=np.float32)
    return out


# version 12
# speedup vs baseline: 1.1688x; 1.1688x over previous
import numpy as np
import ml_dtypes
from contextlib import ExitStack

import concourse.bass as bass
import concourse.mybir as mybir
from concourse.bass_utils import run_bass_kernel_spmd

F32 = mybir.dt.float32
BF16 = mybir.dt.bfloat16
I8 = mybir.dt.int8
AF = mybir.ActivationFunctionType
ALU = mybir.AluOpType
AX = mybir.AxisListType

D = 4096
DC = 32
TKV = 4096
TQ = 2048
PART = 2048
NCT = 16
NI = 11008
NIT = 86
EPS = 1e-5
VW = 272
BIGC = 20000.0

B_MODE = "full"
PROJ_LIMIT = 10 ** 9


def _bs_tiles(bs):
    if bs < 2:
        return ([("K", ct, 16 + ct) for ct in range(NCT)]
                + [("V", ct, 32 + ct) for ct in range(NCT)])
    return [("Q", ct, ct) for ct in range(NCT)]


def _sched():
    tab = []
    for sl in range(32):
        hh, qs = sl // 4, sl % 4
        nks = min(qs + 5, 8)
        nkc = min(4 * qs + 20, 32)
        band0 = 4 * qs
        nav = [min(4 * qs + qt + 17, 32) for qt in range(4)]
        tab.append(dict(hh=hh, qs=qs, nks=nks, nkc=nkc, band0=band0,
                        nband=nkc - band0, nav=nav))
    return tab


SCHED = _sched()
EX_PER = [s["nkc"] for s in SCHED]
AV_PER = [4] * 32
OT_PER = [8] * 32


def _cum(per):
    c = [0]
    for v in per:
        c.append(c[-1] + v)
    return c


EX_CUM = _cum(EX_PER)
AV_CUM = _cum(AV_PER)
OT_CUM = _cum(OT_PER)


def _d_counts():
    vsp = 4
    sex = 0
    REL = []
    SS = [[0] * 32 for _ in range(32)]
    for sl in range(32):
        st = SCHED[sl]
        for qt in range(4):
            for ks in range(st["nks"]):
                vsp += 1
                REL.append(("v", vsp))
                vsp += 1
            vsp += 1
        for kc in range(st["nkc"]):
            if kc >= st["band0"]:
                vsp += 1
                REL.append(("v", vsp))
                SS[sl][kc] = vsp
            else:
                sex += 1
                REL.append(("s", sex))
        sex += st["nband"]
        for qt in range(4):
            vsp += 1
    return REL, SS


D_REL, D_SS = _d_counts()

def _d_counts2():
    vsp = 4
    sex = 0
    REL = []
    SS = [[0] * 32 for _ in range(32)]
    for sl in range(32):
        st = SCHED[sl]
        for qt in range(4):
            for ks in range(st["nks"]):
                vsp += 1
                REL.append(("v", vsp))
                vsp += 1
            vsp += 1
        for kc in range(st["nkc"]):
            sex += 1
            if kc >= st["band0"]:
                vsp += 1
                REL.append(("v", vsp))
                SS[sl][kc] = vsp
            else:
                REL.append(("s", sex))
        for qt in range(4):
            vsp += 1
    return REL, SS


D_REL, D_SS = _d_counts2()

GC = [0]
IUB = []
for _sl in range(32):
    _st = SCHED[_sl]
    IUB.append(GC[-1] + 4 * _st["nks"])
    GC.append(GC[-1] + 4 * _st["nks"] + _st["nkc"])

EQ = [0, 22, 44, 66, NIT]
NT_BS = [32, 32, 16]
TILE_BASE = [0, 32, 64]
GRP_BASE = [0, 128, 256]
TOT_GRP = 320


class _NullBlock:
    def __enter__(self):
        return self

    def __exit__(self, *a):
        return False

    def _skip(self, f):
        return None

    gpsimd = scalar = tensor = vector = sync = _skip


def build_nc(stages="ABCDE", dbg=False):
    nc = bass.Bass("TRN2", target_bir_lowering=False)

    xT = nc.dram_tensor("xT", [D, TKV], F32, kind="ExternalInput")
    xqT = nc.dram_tensor("xqT", [D, TQ], F32, kind="ExternalInput")
    xq = nc.dram_tensor("xq", [TQ, D], F32, kind="ExternalInput")
    wqkvP = nc.dram_tensor("wqkvP", [48, 128, DC * 128], F32, kind="ExternalInput")
    woP = nc.dram_tensor("woP", [8, 128, NCT * 512], F32, kind="ExternalInput")
    bqkvT = nc.dram_tensor("bqkvT", [128, 48], F32, kind="ExternalInput")
    bo = nc.dram_tensor("bo", [1, D], F32, kind="ExternalInput")
    wgP = nc.dram_tensor("wgP", [NIT, 128, DC * 128], I8, kind="ExternalInput")
    wuP = nc.dram_tensor("wuP", [NIT, 128, DC * 128], I8, kind="ExternalInput")
    wdP = nc.dram_tensor("wdP", [8, 128, NIT * 512], I8, kind="ExternalInput")
    gs_d = nc.dram_tensor("gs", [128, 1], F32, kind="ExternalInput")
    us_d = nc.dram_tensor("us", [128, 1], F32, kind="ExternalInput")
    ds_d = nc.dram_tensor("ds", [128, 1], F32, kind="ExternalInput")
    off_d = nc.dram_tensor("off", [128, 1], F32, kind="ExternalInput")
    identb_d = nc.dram_tensor("identb", [128, 128], BF16, kind="ExternalInput")
    identf_d = nc.dram_tensor("identf", [128, 128], F32, kind="ExternalInput")
    y = nc.dram_tensor("y", [TQ, D], F32, kind="ExternalOutput")

    KT_sc = nc.dram_tensor("KT_sc", [NCT, 128, TKV], BF16, kind="Internal")
    VT_sc = nc.dram_tensor("VT_sc", [NCT, 128, TKV], BF16, kind="Internal")
    QT_sc = nc.dram_tensor("QT_sc", [NCT, 128, TQ], BF16, kind="Internal")
    P_sc = nc.dram_tensor("P_sc", [16, 128, NIT, 128], BF16, kind="Internal")
    OT_sc = nc.dram_tensor("OT_sc", [16, 128, NCT, 128], BF16, kind="Internal")
    bo_dr = nc.dram_tensor("bo_dr", [1, D], BF16, kind="Internal")

    if dbg:
        dbg_outs = {
            "dKT": nc.dram_tensor("dKT", [NCT, 128, TKV], BF16, kind="ExternalOutput"),
            "dVT": nc.dram_tensor("dVT", [NCT, 128, TKV], BF16, kind="ExternalOutput"),
            "dQT": nc.dram_tensor("dQT", [NCT, 128, TQ], BF16, kind="ExternalOutput"),
            "dP": nc.dram_tensor("dP", [16, 128, NIT, 128], BF16, kind="ExternalOutput"),
            "dOT": nc.dram_tensor("dOT", [16, 128, NCT, 128], BF16, kind="ExternalOutput"),
        }

    with ExitStack() as outer:
        sbo = lambda *a: outer.enter_context(nc.sbuf_tensor(*a))
        semo = lambda n: outer.enter_context(nc.semaphore(n))

        bqkv_s = sbo("bqkv_s", [128, 48], F32)
        gs_s = sbo("gs_s", [128, 1], F32)
        us_s = sbo("us_s", [128, 1], F32)
        ds_s = sbo("ds_s", [128, 1], F32)
        usds = sbo("usds", [128, 1], F32)
        off_s = sbo("off_s", [128, 1], F32)
        identb = sbo("identb_s", [128, 128], BF16)
        identf = sbo("identf_s", [128, 128], F32)
        onesb = sbo("onesb", [128, 1], BF16)
        onesfr = sbo("onesfr", [1, 128], F32)
        onesbr = sbo("onesbr", [1, 128], BF16)
        p_ramp = sbo("p_ramp", [128, 1], F32)
        eps_s = sbo("eps_s", [128, 1], F32)
        negbig_s = sbo("negbig_s", [128, 1], F32)

        dA = semo("dA")
        with ExitStack() as aa:
            bo_f = aa.enter_context(nc.sbuf_tensor("bo_f", [1, D], F32))
            bo_bf = aa.enter_context(nc.sbuf_tensor("bo_bf", [1, D], BF16))
            aA = aa.enter_context(nc.semaphore("aA"))
            aD = aa.enter_context(nc.semaphore("aD"))
            with (nc.Block("setup") if "A" in stages else _NullBlock()) as blk:
                @blk.sync
                def _(s):
                    s.dma_start(bqkv_s[:, :], bqkvT.ap()).then_inc(dA, 16)
                    s.dma_start(gs_s[:, :], gs_d.ap()).then_inc(dA, 16)
                    s.dma_start(us_s[:, :], us_d.ap()).then_inc(dA, 16)
                    s.dma_start(ds_s[:, :], ds_d.ap()).then_inc(dA, 16)
                    s.dma_start(off_s[:, :], off_d.ap()).then_inc(dA, 16)
                    s.dma_start(identb[:, :], identb_d.ap()).then_inc(dA, 16)
                    s.dma_start(identf[:, :], identf_d.ap()).then_inc(dA, 16)
                    s.dma_start(bo_f[:, :], bo.ap()).then_inc(dA, 16)
                    s.wait_ge(dA, 16 * 8)

                @blk.gpsimd
                def _(g):
                    g.memset(onesb[:, :], 1.0)
                    g.memset(eps_s[:, :], EPS)
                    g.memset(negbig_s[:, :], -BIGC)
                    g.memset(onesfr[:, :], 1.0)
                    g.memset(onesbr[:, :], 1.0)
                    g.iota(p_ramp[:, :], [[1, 1]], channel_multiplier=1,
                           allow_small_or_imprecise_dtypes=True)

                @blk.vector
                def _(v):
                    v.wait_ge(dA, 16 * 8)
                    v.tensor_mul(usds[:, :], us_s[:, :], ds_s[:, :])
                    v.tensor_copy(bo_bf[:, :], bo_f[:, :]).then_inc(aA, 1)

                @blk.gpsimd
                def _(g):
                    g.wait_ge(aA, 1)
                    g.dma_start(bo_dr.ap(), bo_bf[:, :]).then_inc(aD, 16)
                    g.wait_ge(aD, 16)

        with ExitStack() as bc:
            xnbig = bc.enter_context(nc.sbuf_tensor("xnbig", [128, DC * 2048], BF16))

            dXs = [semo(f"dX{i}") for i in range(4)]
            dWb = [semo(f"dW{i}") for i in range(4)]
            dSCs = [semo(f"dSC{i}") for i in range(2)]
            sSQ = semo("sSQ")
            sSD = semo("sSD")
            sEV = semo("sEV")
            tSQC = semo("tSQC")
            tREP = semo("tREP")
            tMP = semo("tMP")
            vFIN = semo("vFIN")
            vVAR = semo("vVAR")
            vSTAT = semo("vSTAT")
            vREP = semo("vREP")
            vNORM = semo("vNORM")
            pT1 = semo("pT1")
            vSelfB = semo("vSelfB")

            with ExitStack() as bb:
                sbb = lambda *a: bb.enter_context(nc.sbuf_tensor(*a))
                psb = lambda *a: bb.enter_context(nc.psum_tensor(*a))
                x2t = [sbb(f"x2t{i}", [128, 512], BF16) for i in range(2)]
                t1b = [sbb(f"t1b{i}", [128, 512], F32) for i in range(2)]
                t1v = sbb("t1v", [128, 512], F32)
                stat = [sbb(f"stat{i}", [1, 512], F32) for i in range(7)]
                r_rep = sbb("r_rep", [128, 512], F32)
                mr_rep = sbb("mr_rep", [128, 512], F32)
                wtile = [sbb(f"wtile{i}", [128, DC * 128], BF16) for i in range(4)]
                ev = [sbb(f"ev{i}", [128, 512], BF16) for i in range(2)]
                ps_sum = psb("ps_sum", [1, 512], F32)
                ps_sq = psb("ps_sq", [1, 512], F32)
                rp0 = psb("rp0", [128, 512], F32)
                rp1 = psb("rp1", [128, 512], F32)
                pp = [psb(f"pp{i}", [128, 512], F32) for i in range(4)]

                def xsrc(bs):
                    return xT if bs < 2 else xqT

                xnv = xnbig[:, :].rearrange("p (a t) -> p a t", t=2048)

                with (nc.Block("b_lnqkv") if "B" in stages else _NullBlock()) as blk:
                    @blk.gpsimd
                    def _(g):
                        for bs in range(3):
                            tiles = _bs_tiles(bs)
                            tok0 = bs * 2048 if bs < 2 else 0
                            srcv = xsrc(bs).ap().rearrange("(a p) t -> p a t", p=128)

                            def xdma(ss_):
                                g.dma_start(
                                    xnv[:, :, ss_ * 512:(ss_ + 1) * 512],
                                    srcv[:, :, tok0 + ss_ * 512:
                                         tok0 + (ss_ + 1) * 512],
                                ).then_inc(dXs[ss_], 16)

                            def wdma(j):
                                Tg = TILE_BASE[bs] + j
                                if Tg >= 4 and PROJ_LIMIT >= TOT_GRP:
                                    g.wait_ge(tMP, 4 * (Tg - 3))
                                g.dma_start(wtile[Tg % 4][:, :],
                                            wqkvP.ap()[tiles[j][2]]
                                            ).then_inc(dWb[Tg % 4], 16)

                            if bs > 0 and PROJ_LIMIT >= TOT_GRP:
                                g.wait_ge(tMP, GRP_BASE[bs])
                            xdma(0)
                            wdma(0)
                            wdma(1)
                            xdma(1)
                            wdma(2)
                            xdma(2)
                            wdma(3)
                            xdma(3)
                            for ss_ in range(4):
                                ssg = bs * 4 + ss_
                                g.wait_ge(vREP, ssg + 1)
                                for dc in range(0, DC, 2):
                                    kp = ssg * 16 + dc // 2
                                    if kp >= 2:
                                        g.wait_ge(vNORM, 2 * (kp - 1))
                                    g.tensor_mul(
                                        t1b[kp % 2][:, :],
                                        xnbig[:, dc * 2048 + ss_ * 512:
                                              dc * 2048 + (ss_ + 1) * 512],
                                        r_rep[:, :],
                                    ).then_inc(pT1, 1)
                            for j in range(4, NT_BS[bs]):
                                wdma(j)
                        g.wait_ge(pT1, 192)
                        for i in range(4):
                            g.wait_ge(dXs[i], 16 * 3)
                            g.wait_ge(dWb[i], 16 * 20)

                    @blk.tensor
                    def _(t):
                        grp = 0

                        def proj_group(bs, j, nt):
                            nonlocal grp
                            if grp >= PROJ_LIMIT:
                                return
                            Tg = TILE_BASE[bs] + j
                            t.wait_ge(dWb[Tg % 4], 16 * (Tg // 4 + 1))
                            t.wait_ge(vNORM, 32 * (bs * 4 + nt + 1))
                            if grp >= 4:
                                t.wait_ge(sEV, grp - 3)
                            last = None
                            for dc in range(DC):
                                last = t.matmul(
                                    pp[grp % 4][:, :],
                                    wtile[Tg % 4][:, dc * 128:(dc + 1) * 128],
                                    xnbig[:, dc * 2048 + nt * 512:
                                          dc * 2048 + (nt + 1) * 512],
                                    start=(dc == 0), stop=(dc == DC - 1),
                                )
                            last.then_inc(tMP, 1)
                            grp += 1

                        for bs in range(3):
                            for ss_ in range(4):
                                ssg = bs * 4 + ss_
                                for dc in range(DC):
                                    k = ssg * 32 + dc
                                    if dc == 0 and ssg > 0:
                                        t.wait_ge(vFIN, ssg)
                                    if dc == 0:
                                        t.wait_ge(dXs[ss_], 16 * (bs + 1))
                                    t.wait_ge(sSQ, k + 1)
                                    t.matmul(ps_sum[:, :], onesb[:, :],
                                             xnbig[:, dc * 2048 + ss_ * 512:
                                                   dc * 2048 + (ss_ + 1) * 512],
                                             start=(dc == 0), stop=(dc == DC - 1))
                                    t.matmul(ps_sq[:, :], onesb[:, :],
                                             x2t[k % 2][:, :],
                                             start=(dc == 0), stop=(dc == DC - 1),
                                             ).then_inc(tSQC, 1)
                                t.wait_ge(vSTAT, ssg + 1)
                                if ssg > 0:
                                    t.wait_ge(vREP, ssg)
                                t.matmul(rp0[:, :], onesfr[:, :], stat[5][0:1, :],
                                         start=True, stop=True)
                                t.matmul(rp1[:, :], onesfr[:, :], stat[6][0:1, :],
                                         start=True, stop=True).then_inc(tREP, 1)
                                proj_group(bs, 0, ss_)
                            for j in range(1, NT_BS[bs]):
                                for nt in range(4):
                                    proj_group(bs, j, nt)
                        assert grp == min(TOT_GRP, PROJ_LIMIT)

                    @blk.scalar
                    def _(s):
                        grp = 0

                        def evac(j, nt, tiles):
                            nonlocal grp
                            if grp >= PROJ_LIMIT:
                                grp += 1
                                return
                            s.wait_ge(tMP, grp + 1)
                            if grp >= 2:
                                s.wait_ge(dSCs[grp % 2], 16 * ((grp - 2) // 2 + 1))
                            bias_col = tiles[j][2]
                            s.activation(ev[grp % 2][:, :], pp[grp % 4][:, :],
                                         AF.Identity,
                                         bias=bqkv_s[:, bias_col:bias_col + 1],
                                         scale=1.0).then_inc(sEV, 1)
                            grp += 1

                        for bs in range(3):
                            tiles = _bs_tiles(bs)
                            for ss_ in range(4):
                                ssg = bs * 4 + ss_
                                for dc in range(DC):
                                    k = ssg * 32 + dc
                                    if dc == 0:
                                        s.wait_ge(dXs[ss_], 16 * (bs + 1))
                                    if k >= 2:
                                        s.wait_ge(tSQC, k - 1)
                                    s.activation(x2t[k % 2][:, :],
                                                 xnbig[:, dc * 2048 + ss_ * 512:
                                                       dc * 2048 + (ss_ + 1) * 512],
                                                 AF.Square).then_inc(sSQ, 1)
                                s.wait_ge(vVAR, ssg + 1)
                                s.activation(stat[4][0:1, :], stat[3][0:1, :],
                                             AF.Sqrt, bias=eps_s[0:1, 0:1],
                                             scale=1.0).then_inc(sSD, 1)
                                evac(0, ss_, tiles)
                            for j in range(1, NT_BS[bs]):
                                for nt in range(4):
                                    evac(j, nt, tiles)

                    @blk.vector
                    def _(v):
                        vsfB = 0
                        for bs in range(3):
                            for ss_ in range(4):
                                ssg = bs * 4 + ss_
                                v.wait_ge(tSQC, 32 * (ssg + 1))
                                if ssg > 0:
                                    v.wait_ge(sSD, ssg)
                                    v.wait_ge(tREP, ssg)
                                    v.wait_ge(vSTAT, ssg)
                                    v.wait_ge(vVAR, ssg)
                                v.tensor_scalar(stat[0][0:1, :], ps_sum[:, :],
                                                1.0 / D, None, ALU.mult
                                                ).then_inc(vSelfB, 1)
                                vsfB += 1
                                v.tensor_scalar(stat[1][0:1, :], ps_sq[:, :],
                                                1.0 / D, None, ALU.mult
                                                ).then_inc(vFIN, 1)
                                v.wait_ge(vSelfB, vsfB)
                                v.wait_ge(vFIN, ssg + 1)
                                v.tensor_mul(stat[2][0:1, :], stat[0][0:1, :],
                                             stat[0][0:1, :]).then_inc(vSelfB, 1)
                                vsfB += 1
                                v.wait_ge(vSelfB, vsfB)
                                v.tensor_sub(stat[3][0:1, :], stat[1][0:1, :],
                                             stat[2][0:1, :]).then_inc(vVAR, 1)
                                v.wait_ge(sSD, ssg + 1)
                                v.reciprocal(stat[5][0:1, :], stat[4][0:1, :]
                                             ).then_inc(vSelfB, 1)
                                vsfB += 1
                                v.wait_ge(vSelfB, vsfB)
                                v.tensor_mul(stat[6][0:1, :], stat[0][0:1, :],
                                             stat[5][0:1, :]).then_inc(vSTAT, 1)
                                v.wait_ge(tREP, ssg + 1)
                                if ssg > 0:
                                    v.wait_ge(pT1, 16 * ssg)
                                v.tensor_copy(r_rep[:, :], rp0[:, :])
                                v.tensor_copy(mr_rep[:, :], rp1[:, :]
                                              ).then_inc(vREP, 1)
                                v.wait_ge(vREP, ssg + 1)
                                for dc in range(DC):
                                    k = ssg * 32 + dc
                                    if dc % 2 == 0:
                                        kp = ssg * 16 + dc // 2
                                        v.wait_ge(pT1, kp + 1)
                                        v.tensor_sub(
                                            xnbig[:, dc * 2048 + ss_ * 512:
                                                  dc * 2048 + (ss_ + 1) * 512],
                                            t1b[kp % 2][:, :], mr_rep[:, :]
                                        ).then_inc(vNORM, 1)
                                    else:
                                        v.tensor_mul(
                                            t1v[:, :],
                                            xnbig[:, dc * 2048 + ss_ * 512:
                                                  dc * 2048 + (ss_ + 1) * 512],
                                            r_rep[:, :]).then_inc(vSelfB, 1)
                                        vsfB += 1
                                        v.wait_ge(vSelfB, vsfB)
                                        v.tensor_sub(
                                            xnbig[:, dc * 2048 + ss_ * 512:
                                                  dc * 2048 + (ss_ + 1) * 512],
                                            t1v[:, :], mr_rep[:, :]
                                        ).then_inc(vNORM, 1)

                    @blk.sync
                    def _(s):
                        grp = 0
                        for bs in range(3):
                            tiles = _bs_tiles(bs)
                            tok0 = bs * 2048 if bs < 2 else 0
                            order = [(0, nt) for nt in range(4)] + [
                                (j, nt) for j in range(1, NT_BS[bs])
                                for nt in range(4)]
                            for (j, nt) in order:
                                if grp >= PROJ_LIMIT:
                                    grp += 1
                                    continue
                                s.wait_ge(sEV, grp + 1)
                                kind, ct, _ = tiles[j]
                                dst = {"K": KT_sc, "V": VT_sc, "Q": QT_sc}[kind]
                                c0 = (tok0 if kind != "Q" else 0) + nt * 512
                                s.dma_start(dst.ap()[ct][:, c0:c0 + 512],
                                            ev[grp % 2][:, :]
                                            ).then_inc(dSCs[grp % 2], 16)
                                grp += 1
                        s.wait_ge(dSCs[0], 16 * 160)
                        s.wait_ge(dSCs[1], 16 * 160)

            dWGs = [semo(f"dWG{i}") for i in range(2)]
            dWUs = [semo(f"dWU{i}") for i in range(2)]
            dPs = [semo(f"dP{i}") for i in range(3)]
            tG = semo("tG")
            tU = semo("tU")
            sSI = semo("sSI")
            vUT = semo("vUT")
            vPT = semo("vPT")
            NG = NIT * 4

            with ExitStack() as ccx:
                sbc = lambda *a: ccx.enter_context(nc.sbuf_tensor(*a))
                psc = lambda *a: ccx.enter_context(nc.psum_tensor(*a))
                wg_bf = [sbc(f"wg_bf{i}", [128, DC * 128], BF16) for i in range(2)]
                wu_bf = [sbc(f"wu_bf{i}", [128, DC * 128], BF16) for i in range(2)]
                gt = [sbc(f"gt{i}", [128, 512], BF16) for i in range(3)]
                ut = [sbc(f"ut{i}", [128, 512], BF16) for i in range(3)]
                ptile = [sbc(f"ptile{i}", [128, 512], BF16) for i in range(3)]
                g_ps = [psc(f"g_ps{i}", [128, 512], F32) for i in range(3)]
                u_ps = [psc(f"u_ps{i}", [128, 512], F32) for i in range(3)]

                with (nc.Block("c_ffn") if "C" in stages else _NullBlock()) as blk:
                    @blk.gpsimd
                    def _(g):
                        for it in range(NIT):
                            if it >= 2:
                                g.wait_ge(tU, 4 * (it - 1))
                            g.dma_start(wg_bf[it % 2][:, :], wgP.ap()[it]
                                        ).then_inc(dWGs[it % 2], 16)
                            g.dma_start(wu_bf[it % 2][:, :], wuP.ap()[it]
                                        ).then_inc(dWUs[it % 2], 16)
                        g.wait_ge(dWGs[0], 16 * 43)
                        g.wait_ge(dWGs[1], 16 * 43)
                        g.wait_ge(dWUs[0], 16 * 43)
                        g.wait_ge(dWUs[1], 16 * 43)

                    @blk.tensor
                    def _(t):
                        for it in range(NIT):
                            t.wait_ge(dWGs[it % 2], 16 * (it // 2 + 1))
                            t.wait_ge(dWUs[it % 2], 16 * (it // 2 + 1))
                            for nt in range(4):
                                gidx = it * 4 + nt
                                if gidx >= 3:
                                    t.wait_ge(sSI, gidx - 2)
                                last = None
                                for dc in range(DC):
                                    last = t.matmul(
                                        g_ps[gidx % 3][:, :],
                                        wg_bf[it % 2][:, dc * 128:(dc + 1) * 128],
                                        xnbig[:, dc * 2048 + nt * 512:
                                              dc * 2048 + (nt + 1) * 512],
                                        start=(dc == 0), stop=(dc == DC - 1))
                                last.then_inc(tG, 1)
                                if gidx >= 3:
                                    t.wait_ge(vUT, gidx - 2)
                                for dc in range(DC):
                                    last = t.matmul(
                                        u_ps[gidx % 3][:, :],
                                        wu_bf[it % 2][:, dc * 128:(dc + 1) * 128],
                                        xnbig[:, dc * 2048 + nt * 512:
                                              dc * 2048 + (nt + 1) * 512],
                                        start=(dc == 0), stop=(dc == DC - 1))
                                last.then_inc(tU, 1)

                    @blk.scalar
                    def _(s):
                        for gidx in range(NG):
                            s.wait_ge(tG, gidx + 1)
                            if gidx >= 3:
                                s.wait_ge(vPT, gidx - 2)
                            s.activation(gt[gidx % 3][:, :], g_ps[gidx % 3][:, :],
                                         AF.Silu, bias=0.0, scale=gs_s[:, 0:1]
                                         ).then_inc(sSI, 1)

                    @blk.vector
                    def _(v):
                        for gidx in range(NG):
                            v.wait_ge(tU, gidx + 1)
                            if gidx >= 3:
                                v.wait_ge(vPT, gidx - 2)
                            v.tensor_scalar(ut[gidx % 3][:, :],
                                            u_ps[gidx % 3][:, :],
                                            usds[:, 0:1], None, ALU.mult
                                            ).then_inc(vUT, 1)
                            v.wait_ge(vUT, gidx + 1)
                            v.wait_ge(sSI, gidx + 1)
                            if gidx >= 3:
                                v.wait_ge(dPs[gidx % 3], 16 * ((gidx - 3) // 3 + 1))
                            v.tensor_mul(ptile[gidx % 3][:, :], gt[gidx % 3][:, :],
                                         ut[gidx % 3][:, :]).then_inc(vPT, 1)

                    @blk.sync
                    def _(s):
                        for it in range(NIT):
                            for nt in range(4):
                                gidx = it * 4 + nt
                                s.wait_ge(vPT, gidx + 1)
                                dst = bass.AP(
                                    P_sc, (nt * 4) * 128 * NIT * 128 + it * 128,
                                    [[NIT * 128, 128], [128 * NIT * 128, 4],
                                     [1, 128]])
                                s.dma_start(dst,
                                            ptile[gidx % 3][:, :].rearrange(
                                                "p (a u) -> p a u", u=128)
                                            ).then_inc(dPs[gidx % 3], 16)
                        s.wait_ge(dPs[0], 16 * 115)
                        s.wait_ge(dPs[1], 16 * 115)
                        s.wait_ge(dPs[2], 16 * 114)

        dQs = [semo(f"dQ{i}") for i in range(2)]
        dKs = [semo(f"dK{i}") for i in range(2)]
        dVs = [semo(f"dV{i}") for i in range(2)]
        dOTs = [semo(f"dOT{i}") for i in range(2)]
        gMS = semo("gMS")
        tS = semo("tS")
        tMT = semo("tMT")
        tRP = semo("tRP")
        tAV = semo("tAV")
        tTR = semo("tTR")
        sMR = semo("sMR")
        sEX = semo("sEX")
        sOT = semo("sOT")
        vSP = semo("vSP")
        vM4 = semo("vM4")
        vRP = semo("vRP")
        vO = semo("vO")

        with ExitStack() as dd:
            sbd = lambda *a: dd.enter_context(nc.sbuf_tensor(*a))
            psd = lambda *a: dd.enter_context(nc.psum_tensor(*a))
            qt_buf = [sbd(f"qt_buf{i}", [128, 2 * TQ], BF16) for i in range(2)]
            kt_buf = [sbd(f"kt_buf{i}", [128, 2 * TKV], BF16) for i in range(2)]
            vaug = [sbd(f"vaug{i}", [128, 32 * VW], BF16) for i in range(2)]
            ptil = sbd("ptil", [128, 32 * 512], BF16)
            km_tab = sbd("km_tab", [128, 3072], BF16)
            qm_tab = sbd("qm_tab", [128, 6144], BF16)
            itmp = sbd("itmp", [128, 6144], F32)
            stat_tmp = [sbd(f"stat_tmp{i}", [128, 512], F32) for i in range(2)]
            ssubB = [sbd(f"ssubB{i}", [128, 512], F32) for i in range(2)]
            mcols = sbd("mcols", [128, 8], F32)
            m4 = sbd("m4", [128, 4], F32)
            nm_bf = sbd("nm_bf", [1, 512], BF16)
            thr2 = sbd("thr2", [128, 1], F32)
            rl = [sbd(f"rl{i}", [128, 1], F32) for i in range(2)]
            o_sb = [sbd(f"o_sb{i}", [128, 256], BF16) for i in range(2)]
            ot_stg = [sbd(f"ot_stg{i}", [128, 128], BF16) for i in range(2)]
            s_ps = [psd(f"s_ps{i}", [128, 512], F32) for i in range(3)]
            o_ps = [psd(f"o_ps{i}", [128, 257], F32) for i in range(2)]
            mrow_ps = psd("mrow_ps", [1, 512], F32)
            tr_ps = [psd(f"tr_ps{i}", [128, 1024], BF16) for i in range(2)]

            with (nc.Block("d_attn") if "D" in stages else _NullBlock()) as blk:
                @blk.gpsimd
                def _(g):
                    g.iota(itmp[:, :], [[1, 6144]], base=-1920,
                           channel_multiplier=0,
                           allow_small_or_imprecise_dtypes=True
                           ).then_inc(gMS, 1)
                    g.wait_ge(vSP, 2)
                    g.iota(itmp[:, 0:3072], [[1, 3072]], base=-2432,
                           channel_multiplier=0,
                           allow_small_or_imprecise_dtypes=True
                           ).then_inc(gMS, 1)
                    g.memset(vaug[0][:, :], 0.0).then_inc(gMS, 1)
                    g.memset(vaug[1][:, :], 0.0).then_inc(gMS, 1)
                    g.wait_ge(gMS, 4)
                    g.memset(vaug[0][:, :].rearrange(
                        "p (a u) -> p a u", u=VW)[:, :, 256:257], 1.0
                        ).then_inc(gMS, 1)
                    g.memset(vaug[1][:, :].rearrange(
                        "p (a u) -> p a u", u=VW)[:, :, 256:257], 1.0
                        ).then_inc(gMS, 1)

                @blk.sync
                def _(s):
                    for hh in range(8):
                        if hh > 0:
                            s.wait_ge(tAV, AV_CUM[4 * (hh - 1)])
                        s.dma_start(
                            qt_buf[hh % 2][:, :]
                            .rearrange("p (a t) -> p a t", t=TQ),
                            QT_sc.ap()[2 * hh:2 * hh + 2]
                            .rearrange("a p t -> p a t"),
                        ).then_inc(dQs[hh % 2], 16)
                        s.dma_start(
                            kt_buf[hh % 2][:, :]
                            .rearrange("p (a t) -> p a t", t=TKV),
                            KT_sc.ap()[2 * hh:2 * hh + 2]
                            .rearrange("a p t -> p a t"),
                        ).then_inc(dKs[hh % 2], 16)
                    for i in range(2):
                        s.wait_ge(dQs[i], 16 * 4)
                        s.wait_ge(dKs[i], 16 * 4)

                @blk.scalar
                def _(s):
                    otk = 0
                    for hh in range(8):
                        if hh == 0:
                            s.wait_ge(gMS, 6)
                        else:
                            s.wait_ge(tAV, AV_CUM[4 * (hh - 1)])
                        s.dma_start(
                            vaug[hh % 2][:, :].rearrange(
                                "p (a u) -> p a u", u=VW)[:, :, 0:256],
                            VT_sc.ap()[2 * hh:2 * hh + 2]
                            .rearrange("a p t -> (a p) t"),
                            transpose=True,
                        ).then_inc(dVs[hh % 2], 16)
                        for qs in range(4):
                            sl = hh * 4 + qs
                            st = SCHED[sl]
                            s.wait_ge(tMT, sl + 1)
                            if sl > 0:
                                s.wait_ge(tS, GC[sl])
                            s.copy(nm_bf[0:1, :], mrow_ps[:, :]).then_inc(sMR, 1)
                            for kc in range(st["nkc"]):
                                if sl > 0 and kc == 0:
                                    s.wait_ge(tAV, AV_CUM[sl])
                                if kc >= st["band0"]:
                                    s.wait_ge(vSP, D_SS[sl][kc])
                                    s.activation(ptil[:, kc * 512:(kc + 1) * 512],
                                                 ssubB[kc % 2][:, :], AF.Exp,
                                                 bias=negbig_s[:, 0:1]
                                                 ).then_inc(sEX, 1)
                                else:
                                    iu_kc = IUB[sl] + kc
                                    s.wait_ge(tS, iu_kc + 1)
                                    s.activation(ptil[:, kc * 512:(kc + 1) * 512],
                                                 s_ps[iu_kc % 3][:, :], AF.Exp
                                                 ).then_inc(sEX, 1)
                            for qt in range(4):
                                for j2 in range(2):
                                    s.wait_ge(tTR, OT_CUM[sl] + qt * 2 + j2 + 1)
                                    if otk >= 2:
                                        s.wait_ge(dOTs[otk % 2],
                                                  16 * ((otk - 2) // 2 + 1))
                                    s.copy(ot_stg[otk % 2][:, :],
                                           tr_ps[otk % 2][:, 0:128]
                                           ).then_inc(sOT, 1)
                                    s.wait_ge(sOT, otk + 1)
                                    cc = 2 * hh + j2
                                    tt = qs * 4 + qt
                                    s.dma_start(
                                        bass.AP(OT_sc,
                                                tt * 128 * NCT * 128 + cc * 128,
                                                [[NCT * 128, 128], [1, 128]]),
                                        ot_stg[otk % 2][:, :]
                                        ).then_inc(dOTs[otk % 2], 16)
                                    otk += 1
                    s.wait_ge(dOTs[0], 16 * 128)
                    s.wait_ge(dOTs[1], 16 * 128)
                    s.wait_ge(dVs[0], 16 * 4)
                    s.wait_ge(dVs[1], 16 * 4)

                @blk.tensor
                def _(t):
                    iu = 0
                    gO = 0
                    ktr = 0
                    for hh in range(8):
                        t.wait_ge(dQs[hh % 2], 16 * (hh // 2 + 1))
                        t.wait_ge(dKs[hh % 2], 16 * (hh // 2 + 1))
                        for qs in range(4):
                            sl = hh * 4 + qs
                            st = SCHED[sl]
                            for qt in range(4):
                                for ks in range(st["nks"]):
                                    if iu >= 3:
                                        k, c = D_REL[iu - 3]
                                        t.wait_ge(vSP if k == "v" else sEX, c)
                                    last = None
                                    for j in range(2):
                                        last = t.matmul(
                                            s_ps[iu % 3][:, :],
                                            qt_buf[hh % 2][:,
                                                           j * TQ + qs * 512
                                                           + qt * 128:
                                                           j * TQ + qs * 512
                                                           + (qt + 1) * 128],
                                            kt_buf[hh % 2][:,
                                                           j * TKV + ks * 512:
                                                           j * TKV
                                                           + (ks + 1) * 512],
                                            start=(j == 0), stop=(j == 1))
                                    last.then_inc(tS, 1)
                                    iu += 1
                            t.wait_ge(vM4, 4 * (sl + 1))
                            if sl > 0:
                                t.wait_ge(sMR, sl)
                            last = None
                            for qt in range(4):
                                last = t.transpose(
                                    mrow_ps[0:1, qt * 128:(qt + 1) * 128],
                                    m4[:, qt:qt + 1], identf[:, :])
                            last.then_inc(tMT, 1)
                            t.wait_ge(sMR, sl + 1)
                            for kc in range(st["nkc"]):
                                if iu >= 3:
                                    k, c = D_REL[iu - 3]
                                    t.wait_ge(vSP if k == "v" else sEX, c)
                                for j in range(2):
                                    t.matmul(
                                        s_ps[iu % 3][:, :],
                                        kt_buf[hh % 2][:, j * TKV + kc * 128:
                                                       j * TKV + (kc + 1) * 128],
                                        qt_buf[hh % 2][:, j * TQ + qs * 512:
                                                       j * TQ + (qs + 1) * 512],
                                        start=(j == 0), stop=False)
                                t.matmul(s_ps[iu % 3][:, :], onesbr[:, :],
                                         nm_bf[0:1, :], start=False, stop=True
                                         ).then_inc(tS, 1)
                                iu += 1
                            t.wait_ge(sEX, EX_CUM[sl + 1])
                            t.wait_ge(dVs[hh % 2], 16 * (hh // 2 + 1))
                            for qt in range(4):
                                if gO >= 2:
                                    t.wait_ge(vO, gO - 1)
                                last = None
                                for kc in range(st["nav"][qt]):
                                    last = t.matmul(
                                        o_ps[gO % 2][:, :],
                                        ptil[:, kc * 512 + qt * 128:
                                             kc * 512 + (qt + 1) * 128],
                                        vaug[hh % 2][:, kc * VW: kc * VW + 257],
                                        start=(kc == 0),
                                        stop=(kc == st["nav"][qt] - 1))
                                last.then_inc(tAV, 1)
                                gO += 1
                                t.wait_ge(vO, gO)
                                for j2 in range(2):
                                    if ktr >= 2:
                                        t.wait_ge(sOT, ktr - 1)
                                    t.transpose(tr_ps[ktr % 2][:, 0:128],
                                                o_sb[qt % 2][:, j2 * 128:
                                                             (j2 + 1) * 128],
                                                identb[:, :]).then_inc(tTR, 1)
                                    ktr += 1

                @blk.vector
                def _(v):
                    iu = 0
                    gO = 0
                    vsp = 0
                    v.wait_ge(gMS, 1)
                    v.tensor_scalar(thr2[:, :], p_ramp[:, :], 0.0,
                                    off_s[:, 0:1], ALU.add, ALU.add
                                    ).then_inc(vSP, 1)
                    vsp += 1
                    v.wait_ge(vSP, vsp)
                    v.tensor_scalar(qm_tab[:, :], itmp[:, :], thr2[:, 0:1],
                                    None, ALU.is_le).then_inc(vSP, 1)
                    vsp += 1
                    v.wait_ge(vSP, vsp)
                    v.wait_ge(gMS, 2)
                    v.tensor_scalar(thr2[:, :], p_ramp[:, :], 0.0,
                                    off_s[:, 0:1], ALU.add, ALU.subtract
                                    ).then_inc(vSP, 1)
                    vsp += 1
                    v.wait_ge(vSP, vsp)
                    v.tensor_scalar(km_tab[:, :], itmp[:, 0:3072], thr2[:, 0:1],
                                    None, ALU.is_ge).then_inc(vSP, 1)
                    vsp += 1
                    v.wait_ge(vSP, vsp)
                    assert vsp == 4
                    for hh in range(8):
                        for qs in range(4):
                            sl = hh * 4 + qs
                            st = SCHED[sl]
                            for qt in range(4):
                                tt16 = qs * 4 + qt
                                for ks in range(st["nks"]):
                                    qm0 = 1920 + ks * 512 - tt16 * 128
                                    v.wait_ge(tS, iu + 1)
                                    if vsp >= 3:
                                        v.wait_ge(vSP, vsp - 2)
                                    v.scalar_tensor_tensor(
                                        stat_tmp[ks % 2][:, :],
                                        s_ps[iu % 3][:, :],
                                        BIGC,
                                        qm_tab[:, qm0:qm0 + 512],
                                        ALU.add, ALU.mult).then_inc(vSP, 1)
                                    vsp += 1
                                    assert D_REL[iu] == ("v", vsp)
                                    iu += 1
                                    v.wait_ge(vSP, vsp)
                                    v.tensor_reduce(mcols[:, ks:ks + 1],
                                                    stat_tmp[ks % 2][:, :],
                                                    AX.X, ALU.max).then_inc(vSP, 1)
                                    vsp += 1
                                v.wait_ge(vSP, vsp)
                                if sl > 0 and qt == 0:
                                    v.wait_ge(tMT, sl)
                                v.tensor_reduce(m4[:, qt:qt + 1],
                                                mcols[:, 0:st["nks"]],
                                                AX.X, ALU.max, negate=True
                                                ).then_inc(vSP, 1)
                                vsp += 1
                                v.wait_ge(vSP, vsp)
                                v.tensor_scalar(m4[:, qt:qt + 1],
                                                m4[:, qt:qt + 1], BIGC, None,
                                                ALU.add).then_inc(vM4, 1)
                            for kc in range(st["nkc"]):
                                if kc >= st["band0"]:
                                    bi = kc - st["band0"]
                                    km0 = 2432 - bi * 128
                                    v.wait_ge(tS, iu + 1)
                                    if kc >= 2:
                                        v.wait_ge(sEX, EX_CUM[sl] + kc - 1)
                                    elif sl > 0:
                                        v.wait_ge(sEX, EX_CUM[sl])
                                    v.scalar_tensor_tensor(
                                        ssubB[kc % 2][:, :],
                                        s_ps[iu % 3][:, :], BIGC,
                                        km_tab[:, km0:km0 + 512],
                                        ALU.add, ALU.mult).then_inc(vSP, 1)
                                    vsp += 1
                                    assert vsp == D_SS[sl][kc]
                                    assert D_REL[iu] == ("v", vsp)
                                iu += 1
                            for qt in range(4):
                                v.wait_ge(tAV, gO + 1)
                                if gO >= 2:
                                    v.wait_ge(tTR, 2 * (gO - 1))
                                if gO >= 2:
                                    v.wait_ge(vO, gO - 1)
                                v.reciprocal(rl[gO % 2][:, :],
                                             o_ps[gO % 2][:, 256:257]
                                             ).then_inc(vSP, 1)
                                vsp += 1
                                v.wait_ge(vSP, vsp)
                                v.tensor_scalar(o_sb[qt % 2][:, :],
                                                o_ps[gO % 2][:, 0:256],
                                                rl[gO % 2][:, 0:1], None,
                                                ALU.mult).then_inc(vO, 1)
                                gO += 1

        dWDs = [semo(f"dWD{i}") for i in range(3)]
        dWOx = semo("dWOx")
        dBO = semo("dBO")
        dPPs = [semo(f"dPP{i}") for i in range(3)]
        dOPs = [semo(f"dOP{i}") for i in range(2)]
        dXQs = [semo(f"dXQ{i}") for i in range(2)]
        dYs = [semo(f"dY{i}") for i in range(2)]
        tE = semo("tE")
        vE = semo("vE")

        with ExitStack() as ee:
            sbe = lambda *a: ee.enter_context(nc.sbuf_tensor(*a))
            pse = lambda *a: ee.enter_context(nc.psum_tensor(*a))
            wdh = [sbe(f"wdh{i}", [128, 43 * 512], BF16) for i in range(3)]
            wo_bf = sbe("wo_bf", [128, NCT * 512], BF16)
            bo_sl = sbe("bo_sl", [1, 512], BF16)
            p_pan = [sbe(f"p_pan{i}", [128, 43 * 128], BF16) for i in range(3)]
            ot_pan = [sbe(f"ot_pan{i}", [128, NCT * 128], BF16) for i in range(2)]
            xq_t = [sbe(f"xq_t{i}", [128, 512], F32) for i in range(2)]
            out_sb = [sbe(f"out_sb{i}", [128, 512], F32) for i in range(2)]
            out_ps = [pse(f"out_ps{i}", [128, 512], F32) for i in range(3)]

            with (nc.Block("e_out") if "E" in stages else _NullBlock()) as blk:
                @blk.gpsimd
                def _(g):
                    for dsl in range(8):
                        if dsl > 0:
                            g.wait_ge(tE, 16 * (dsl - 1) + 1)
                        h2 = 2 * dsl
                        g.dma_start(wdh[h2 % 3][:, :],
                                    wdP.ap()[dsl][:, 0:43 * 512]
                                    ).then_inc(dWDs[h2 % 3], 16)
                        if dsl > 0:
                            g.wait_ge(tE, 16 * dsl)
                        h2 = 2 * dsl + 1
                        g.dma_start(wdh[h2 % 3][:, :],
                                    wdP.ap()[dsl][:, 43 * 512:86 * 512]
                                    ).then_inc(dWDs[h2 % 3], 16)
                        g.dma_start(wo_bf[:, :], woP.ap()[dsl]).then_inc(dWOx, 16)
                        g.dma_start(bo_sl[:, :],
                                    bo_dr.ap()[:, dsl * 512:(dsl + 1) * 512]
                                    ).then_inc(dBO, 16)
                    g.wait_ge(dWDs[0], 16 * 6)
                    g.wait_ge(dWDs[1], 16 * 5)
                    g.wait_ge(dWDs[2], 16 * 5)
                    g.wait_ge(dWOx, 16 * 8)
                    g.wait_ge(dBO, 16 * 8)

                @blk.sync
                def _(s):
                    for gi in range(128):
                        dsl, tt = gi // 16, gi % 16
                        src_flat = P_sc.ap()[tt].rearrange("p a u -> p (a u)")
                        if gi >= 2:
                            s.wait_ge(tE, gi - 1)
                        s.dma_start(p_pan[(2 * gi) % 3][:, :],
                                    src_flat[:, 0:43 * 128]
                                    ).then_inc(dPPs[(2 * gi) % 3], 16)
                        if gi >= 1:
                            s.wait_ge(tE, gi)
                        s.dma_start(p_pan[(2 * gi + 1) % 3][:, :],
                                    src_flat[:, 43 * 128:NIT * 128]
                                    ).then_inc(dPPs[(2 * gi + 1) % 3], 16)
                        if gi >= 1:
                            pg = gi - 1
                            pdsl, ptt = pg // 16, pg % 16
                            s.wait_ge(vE, pg + 1)
                            s.dma_start(
                                y.ap()[ptt * 128:(ptt + 1) * 128,
                                       pdsl * 512:(pdsl + 1) * 512],
                                out_sb[pg % 2][:, :]).then_inc(dYs[pg % 2], 16)
                    s.wait_ge(vE, 128)
                    s.dma_start(y.ap()[15 * 128:16 * 128, 7 * 512:8 * 512],
                                out_sb[127 % 2][:, :]).then_inc(dYs[1], 16)
                    for i in range(3):
                        s.wait_ge(dPPs[i], 16 * (86 if i == 0 else 85))
                    s.wait_ge(dYs[0], 16 * 64)
                    s.wait_ge(dYs[1], 16 * 64)

                @blk.scalar
                def _(s):
                    for gi in range(128):
                        dsl, tt = gi // 16, gi % 16
                        if gi >= 2:
                            s.wait_ge(tE, gi - 1)
                        s.dma_start(
                            ot_pan[gi % 2][:, :],
                            OT_sc.ap()[tt].rearrange("p a u -> p (a u)"),
                        ).then_inc(dOPs[gi % 2], 16)
                        if gi >= 2:
                            s.wait_ge(vE, gi - 1)
                        s.dma_start(
                            xq_t[gi % 2][:, :],
                            xq.ap()[tt * 128:(tt + 1) * 128,
                                    dsl * 512:(dsl + 1) * 512],
                        ).then_inc(dXQs[gi % 2], 16)
                    for i in range(2):
                        s.wait_ge(dOPs[i], 16 * 64)
                        s.wait_ge(dXQs[i], 16 * 64)

                @blk.tensor
                def _(t):
                    for gi in range(128):
                        dsl = gi // 16
                        t.wait_ge(dOPs[gi % 2], 16 * (gi // 2 + 1))
                        if gi >= 3:
                            t.wait_ge(vE, gi - 2)
                        ha = wdh[(2 * dsl) % 3]
                        hb = wdh[(2 * dsl + 1) % 3]
                        pa = p_pan[(2 * gi) % 3]
                        pb = p_pan[(2 * gi + 1) % 3]
                        for ic in range(NIT):
                            if ic == 0:
                                t.wait_ge(dWDs[(2 * dsl) % 3],
                                          16 * ((2 * dsl) // 3 + 1))
                                t.wait_ge(dPPs[(2 * gi) % 3],
                                          16 * ((2 * gi) // 3 + 1))
                            elif ic == 43:
                                t.wait_ge(dWDs[(2 * dsl + 1) % 3],
                                          16 * ((2 * dsl + 1) // 3 + 1))
                                t.wait_ge(dPPs[(2 * gi + 1) % 3],
                                          16 * ((2 * gi + 1) // 3 + 1))
                            wsl = (ha[:, ic * 512:(ic + 1) * 512] if ic < 43
                                   else hb[:, (ic - 43) * 512:(ic - 42) * 512])
                            psl = (pa[:, ic * 128:(ic + 1) * 128] if ic < 43
                                   else pb[:, (ic - 43) * 128:(ic - 42) * 128])
                            t.matmul(out_ps[gi % 3][:, :], psl,
                                     wsl, start=(ic == 0), stop=False)
                        t.wait_ge(dWOx, 16 * (dsl + 1))
                        t.wait_ge(dBO, 16 * (dsl + 1))
                        for cc in range(NCT):
                            t.matmul(out_ps[gi % 3][:, :],
                                     ot_pan[gi % 2][:, cc * 128:(cc + 1) * 128],
                                     wo_bf[:, cc * 512:(cc + 1) * 512],
                                     start=False, stop=False)
                        t.matmul(out_ps[gi % 3][:, :], onesbr[:, :],
                                 bo_sl[0:1, :],
                                 start=False, stop=True).then_inc(tE, 1)

                @blk.vector
                def _(v):
                    for gi in range(128):
                        v.wait_ge(tE, gi + 1)
                        v.wait_ge(dXQs[gi % 2], 16 * (gi // 2 + 1))
                        if gi >= 2:
                            v.wait_ge(dYs[gi % 2], 16 * ((gi - 2) // 2 + 1))
                        v.tensor_add(out_sb[gi % 2][:, :], out_ps[gi % 3][:, :],
                                     xq_t[gi % 2][:, :]).then_inc(vE, 1)

    if dbg:
        with nc.semaphore("dDBG") as dDBG, nc.Block("dbgcopy") as blk:
            @blk.sync
            def _(s):
                for name, srct in [("dKT", KT_sc), ("dVT", VT_sc), ("dQT", QT_sc),
                                   ("dP", P_sc), ("dOT", OT_sc)]:
                    s.dma_start(dbg_outs[name].ap(), srct.ap()).then_inc(dDBG, 16)
                s.wait_ge(dDBG, 16 * 5)
    return nc


_NC_CACHE = {}


def _get_nc():
    if "nc" not in _NC_CACHE:
        _NC_CACHE["nc"] = build_nc()
    return _NC_CACHE["nc"]


def _pack_int8(w):
    w = np.ascontiguousarray(np.asarray(w))
    assert w.dtype == np.int32
    return w.view(np.int8)[:, 0::4]


def host_inputs(L0_attn_x, L0_attn_w_qkv, L0_attn_b_qkv, L0_attn_w_o,
                L0_attn_b_o, L0_ffn_gate_w, L0_ffn_gate_s, L0_ffn_up_w,
                L0_ffn_up_s, L0_ffn_down_w, L0_ffn_down_s):
    x = np.asarray(L0_attn_x, dtype=np.float32)
    wqkvT = np.ascontiguousarray(np.asarray(L0_attn_w_qkv, dtype=np.float32).T)
    wqkvP = np.ascontiguousarray(
        wqkvT.reshape(DC, 128, 48, 128).transpose(2, 1, 0, 3)
        .reshape(48, 128, DC * 128))
    woT = np.ascontiguousarray(np.asarray(L0_attn_w_o, dtype=np.float32).T)
    woP = np.ascontiguousarray(
        woT.reshape(NCT, 128, 8, 512).transpose(2, 1, 0, 3)
        .reshape(8, 128, NCT * 512))
    bqkvT = np.ascontiguousarray(
        np.asarray(L0_attn_b_qkv, dtype=np.float32).reshape(48, 128).T)
    bo_h = np.ascontiguousarray(
        np.asarray(L0_attn_b_o, dtype=np.float32).reshape(1, D))
    wgT8 = np.ascontiguousarray(_pack_int8(L0_ffn_gate_w).T)
    wgP = np.ascontiguousarray(
        wgT8.reshape(DC, 128, NIT, 128).transpose(2, 1, 0, 3)
        .reshape(NIT, 128, DC * 128))
    wuT8 = np.ascontiguousarray(_pack_int8(L0_ffn_up_w).T)
    wuP = np.ascontiguousarray(
        wuT8.reshape(DC, 128, NIT, 128).transpose(2, 1, 0, 3)
        .reshape(NIT, 128, DC * 128))
    wdT8 = np.ascontiguousarray(_pack_int8(L0_ffn_down_w).T)
    wdP = np.ascontiguousarray(
        wdT8.reshape(NIT, 128, 8, 512).transpose(2, 1, 0, 3)
        .reshape(8, 128, NIT * 512))
    gs = np.ascontiguousarray(np.broadcast_to(
        np.asarray(L0_ffn_gate_s, dtype=np.float32).reshape(1, 1), (128, 1)))
    us = np.ascontiguousarray(np.broadcast_to(
        np.asarray(L0_ffn_up_s, dtype=np.float32).reshape(1, 1), (128, 1)))
    ds = np.ascontiguousarray(np.broadcast_to(
        np.asarray(L0_ffn_down_s, dtype=np.float32).reshape(1, 1), (128, 1)))
    identb = np.eye(128, dtype=ml_dtypes.bfloat16)
    identf = np.eye(128, dtype=np.float32)

    xTb = [np.ascontiguousarray(x[b].T) for b in range(4)]
    in_maps = []
    for c in range(8):
        b, h = c // 2, c % 2
        in_maps.append(dict(
            xT=xTb[b],
            xqT=np.ascontiguousarray(x[b, h * TQ:(h + 1) * TQ].T),
            xq=np.ascontiguousarray(x[b, h * TQ:(h + 1) * TQ]),
            wqkvP=wqkvP, woP=woP, bqkvT=bqkvT, bo=bo_h,
            wgP=wgP, wuP=wuP, wdP=wdP,
            gs=gs, us=us, ds=ds,
            off=np.full((128, 1), float(h * TQ), np.float32),
            identb=identb, identf=identf,
        ))
    return in_maps


def kernel(L0_attn_x, L0_attn_w_qkv, L0_attn_b_qkv, L0_attn_w_o, L0_attn_b_o,
           L0_ffn_gate_w, L0_ffn_gate_s, L0_ffn_up_w, L0_ffn_up_s,
           L0_ffn_down_w, L0_ffn_down_s):
    in_maps = host_inputs(L0_attn_x, L0_attn_w_qkv, L0_attn_b_qkv, L0_attn_w_o,
                          L0_attn_b_o, L0_ffn_gate_w, L0_ffn_gate_s,
                          L0_ffn_up_w, L0_ffn_up_s, L0_ffn_down_w,
                          L0_ffn_down_s)
    nc = _get_nc()
    res = run_bass_kernel_spmd(nc, in_maps, core_ids=list(range(8)))
    out = np.empty((4, 4096, D), dtype=np.float32)
    for c in range(8):
        b, h = c // 2, c % 2
        out[b, h * TQ:(h + 1) * TQ] = np.asarray(res.results[c]["y"],
                                                 dtype=np.float32)
    return out
